# version 05
# baseline (speedup 1.0000x reference)
import contextlib
import numpy as np
import ml_dtypes
import concourse.bass as bass
import concourse.mybir as mybir
from concourse.bass_utils import run_bass_kernel_spmd

F32 = mybir.dt.float32
BF16 = mybir.dt.bfloat16
AF = mybir.ActivationFunctionType
OP = mybir.AluOpType
AX = mybir.AxisListType

NC = 8
S = 16384
D = 4096
TPC = S // NC
NKC = D // 128
BS = 256
NB = TPC // BS
HA = 32
MH, MP, MN, MG = 64, 64, 128, 8
MC = 128
NMC = TPC // MC
EPS = 1e-5

_bf = ml_dtypes.bfloat16


def _attn_consts():
    base = 1.0 / 2.0 ** (8.0 / HA)
    expo = np.arange(HA, dtype=np.float64) + 1.0
    factor = 1.0 - 1 / (32 - 1 + 1e-05) + 1e-05
    slope = base ** expo * factor
    r = np.arange(BS, dtype=np.float64) + 1.0
    q_decay = np.exp(-slope[:, None] * r[None, :])
    k_decay = np.exp(-slope[:, None] * (BS - r)[None, :])
    diff = r[:, None] - r[None, :]
    dd = np.where(diff >= 0, np.exp(-slope[:, None, None] * diff[None]), 0.0)
    bd = np.exp(-slope * BS)
    return q_decay, k_decay, dd, bd


def _host_prep(inputs):
    perm = np.asarray(inputs["L0_ffn_perm"]).astype(np.int64)
    qd, kd, dd, bdec = _attn_consts()

    x = np.asarray(inputs["L0_attn_a_hidden_states"])[0]
    xp = np.ascontiguousarray(x[:, perm])

    def blkw(Wt, npair):
        return np.ascontiguousarray(
            Wt.reshape(4, 8, 128, npair, 256).transpose(3, 0, 2, 1, 4))

    qkvT = np.asarray(inputs["L0_attn_a_qkv_w"])[:, perm].T
    gateT = np.asarray(inputs["L0_attn_a_gate_w"])[:, perm].T
    qkvg = blkw(np.concatenate([qkvT, gateT], axis=1), 64)
    outw = blkw(np.asarray(inputs["L0_attn_a_out_w"])[perm, :].T, 16)
    inpT = np.asarray(inputs["L0_attn_b_in_proj_w"])[:, perm].T
    inpw = blkw(np.ascontiguousarray(inpT[:, :10240]), 40)
    dtw = np.ascontiguousarray(
        inpT[:, 10240:].reshape(4, 8, 128, 64).transpose(0, 2, 1, 3))
    outpw = blkw(np.asarray(inputs["L0_attn_b_out_proj_w"])[perm, :].T, 16)
    upw = np.ascontiguousarray(np.asarray(inputs["L0_ffn_up"]).reshape(16, 2, 128, 1024))
    dnw = np.ascontiguousarray(np.asarray(inputs["L0_ffn_down"]).reshape(16, 8, 128, 256))

    com = {
        "qkvg_w": qkvg.astype(np.float32),
        "outw_w": outw.astype(np.float32),
        "inpj_w": inpw.astype(np.float32),
        "dtpj_w": dtw.astype(np.float32),
        "outp_w": outpw.astype(np.float32),
        "up_w": upw.astype(np.float32),
        "dn_w": dnw.astype(np.float32),
        "normw_a": np.ascontiguousarray(
            np.asarray(inputs["L0_attn_a_norm_w"]).reshape(32, 128).T).astype(np.float32),
        "normw_b": np.ascontiguousarray(
            np.asarray(inputs["L0_attn_b_norm_weight"]).reshape(32, 128).T).astype(np.float32),
        "convw": np.ascontiguousarray(
            np.asarray(inputs["L0_attn_b_conv_weight"]).reshape(48, 128, 4)).astype(np.float32),
        "convb": np.ascontiguousarray(
            np.asarray(inputs["L0_attn_b_conv_bias"]).reshape(48, 128).T).astype(np.float32),
        "dtbias": np.asarray(inputs["L0_attn_b_dt_bias"]).reshape(64, 1).astype(np.float32),
        "alog": np.asarray(inputs["L0_attn_b_A_log"]).reshape(64, 1).astype(np.float32),
        "d_bc": np.ascontiguousarray(np.broadcast_to(
            np.asarray(inputs["L0_attn_b_D"])[None, :], (128, 64))).astype(np.float32),
        "ddT": np.ascontiguousarray(dd.transpose(0, 2, 1)).astype(np.float32),
        "qd_row": np.ascontiguousarray(np.tile(qd, (1, NB))[:, None, :]).astype(_bf),
        "kd_col": np.ascontiguousarray(
            kd.reshape(HA, 2, 128).transpose(2, 1, 0).reshape(128, 2 * HA)).astype(np.float32),
        "bdpow": np.ascontiguousarray(np.broadcast_to(
            (bdec[:, None] ** np.arange(NB)[None, :]).reshape(1, HA * NB),
            (128, HA * NB))).astype(np.float32),
        "ones_col": np.ones((128, 1), _bf),
        "ones_row": np.ones((1, 128), _bf),
        "ones_row_f32": np.ones((1, 128), np.float32),
        "ones128_f32": np.ones((128, 128), np.float32),
        "ident_bf": np.eye(128, dtype=_bf),
        "ident_f32": np.eye(128, dtype=np.float32),
        "eps_col": np.full((128, 1), EPS, np.float32),
        "one_col": np.ones((128, 1), np.float32),
        "tri_neg": np.where(np.arange(128)[None, :] >= np.arange(128)[:, None],
                            0.0, -1e30).astype(np.float32),
        "tri_negT": np.where(np.arange(128)[None, :] >= np.arange(128)[:, None],
                             0.0, -1e30).astype(np.float32).T.copy(),
    }

    in_maps = []
    for c in range(NC):
        m = dict(com)
        m["x0"] = np.ascontiguousarray(xp[c * TPC:(c + 1) * TPC].T).astype(np.float32)
        wf = np.zeros((NC, HA), np.float64)
        for j in range(c):
            wf[j] = bdec ** (8 * (c - 1 - j))
        m["wfold_a"] = np.ascontiguousarray(np.broadcast_to(
            wf.reshape(1, NC * HA), (128, NC * HA))).astype(np.float32)
        mm = np.zeros((NC, NC), np.float32)
        for j in range(c):
            mm[j + 1:c, j] = 1.0
        m["mmask"] = mm.astype(np.float32)
        jm = np.zeros((NC, 1), np.float32)
        jm[:c] = 1.0
        m["jmask8"] = jm
        hm = np.zeros((128, NC), np.float32)
        if c > 0:
            hm[:, c - 1] = 1.0
        m["halo_mask"] = hm
        in_maps.append(m)
    return in_maps, perm


class Ctr:
    __slots__ = ("h", "n")

    def __init__(self, h):
        self.h = h
        self.n = 0


class KB:
    ENG = ("T", "V", "S", "G", "Y")

    def __init__(self, nc, block):
        self.nc = nc
        self.block = block
        self.ops = {e: [] for e in self.ENG}
        self.ctx = contextlib.ExitStack()
        self.sems = {}
        self.bar_n = 0
        self.own = {}
        self.last_selfwait = {}

    def sem(self, name):
        if name not in self.sems:
            self.sems[name] = Ctr(self.ctx.enter_context(self.nc.semaphore(name)))
        return self.sems[name]

    def op(self, e, fn, inc=None, inck=1):
        if e in ("V", "S"):
            own = self.own[e]
            if inc is None:
                inc = own
            if own.n > 0:
                self.ops[e].append(lambda eng, h=own.h, v=own.n: eng.wait_ge(h, v))
        if inc is not None:
            inc.n += inck

            def wrapped(eng, fn=fn, sem=inc.h, k=inck):
                fn(eng).then_inc(sem, k)
            self.ops[e].append(wrapped)
            return inc.n
        self.ops[e].append(fn)
        return None

    def wait(self, e, ctr, val):
        if val > 0:
            self.ops[e].append(lambda eng, h=ctr.h, v=val: eng.wait_ge(h, v))

    def wait_all(self, e, ctr):
        self.wait(e, ctr, ctr.n)

    def fence(self, e):
        for nm in ("pe", "ve", "ac"):
            if nm in self.sems:
                self.wait_all(e, self.sems[nm])

    def dma(self, e, out_ap, in_ap, ctr):
        return self.op(e, lambda eng, o=out_ap, i=in_ap: eng.dma_start(out=o, in_=i),
                       inc=ctr, inck=16)

    def barrier(self):
        self.bar_n += 1
        n = self.bar_n
        drains = [(nm, c, c.n) for nm, c in self.sems.items()
                  if not nm.startswith("bar_")]
        for e in self.ENG:
            for nm, c, v in drains:
                self.wait(e, c, v)
        for e in self.ENG:
            b = self.sem("bar_" + e)
            self.op(e, lambda eng: eng.nop(), inc=b)
        for e in self.ENG:
            for o in self.ENG:
                if o != e:
                    self.wait(e, self.sem("bar_" + o), n)

    def flush(self):
        ops = self.ops
        self.ops = {e: [] for e in self.ENG}
        if ops["T"]:
            self.block.tensor(lambda t, fns=ops["T"]: [f(t) for f in fns] and None)
        if ops["V"]:
            self.block.vector(lambda v, fns=ops["V"]: [f(v) for f in fns] and None)
        if ops["S"]:
            self.block.scalar(lambda s, fns=ops["S"]: [f(s) for f in fns] and None)
        if ops["G"]:
            self.block.gpsimd(lambda g, fns=ops["G"]: [f(g) for f in fns] and None)
        if ops["Y"]:
            self.block.sync(lambda y, fns=ops["Y"]: [f(y) for f in fns] and None)


DBG_SHAPES = {
    "qkv_sp": ([96, 128, TPC], BF16),
    "gate_a_sp": ([32, 128, TPC], BF16),
    "attn0_sp": ([HA, 16, 128, 128], BF16),
    "qt_sp": ([HA, 128, TPC], BF16),
    "cc_a_src": ([HA * 128, 128], BF16),
    "x1_d": ([D, TPC], BF16),
    "x2_d": ([D, TPC], BF16),
    "x3_d": ([D, TPC], BF16),
    "xbc_sp": ([48, 128, TPC], BF16),
    "gate_b_sp": ([32, 128, TPC], BF16),
    "y_sp": ([16, 128, 4096], BF16),
    "cc_b_src": ([1025, 512], F32),
    "dbg_cum": ([2, 64, TPC], F32),
    "dbg_xt": ([48, 128, 512], BF16),
    "dbg_m": ([5, 128, 1024], BF16),
    "dbg_c": ([128, 256], F32),
}


def build_program(upto="F", debug=()):
    nc = bass.Bass("TRN2", target_bir_lowering=False)

    nc_input_handles = {}

    def din(name, shape, dt=F32):
        h = nc.dram_tensor(name, shape, dt, kind="ExternalInput")
        nc_input_handles[name] = h
        return h.ap()

    x0 = din("x0", [D, TPC])
    qkvg_w = din("qkvg_w", [64, 4, 128, 8, 256])
    outw_w = din("outw_w", [16, 4, 128, 8, 256])
    inpj_w = din("inpj_w", [40, 4, 128, 8, 256])
    dtpj_w = din("dtpj_w", [4, 128, 8, 64])
    outp_w = din("outp_w", [16, 4, 128, 8, 256])
    up_w = din("up_w", [16, 2, 128, 1024])
    dn_w = din("dn_w", [16, 8, 128, 256])
    normw_a = din("normw_a", [128, 32])
    normw_b = din("normw_b", [128, 32])
    convw = din("convw", [48, 128, 4])
    convb = din("convb", [128, 48])
    dtbias = din("dtbias", [64, 1])
    alog = din("alog", [64, 1])
    d_bc = din("d_bc", [128, 64])
    ddT = din("ddT", [HA, 256, 256])
    qd_row = din("qd_row", [HA, 1, TPC], BF16)
    kd_col = din("kd_col", [128, 2 * HA])
    bdpow = din("bdpow", [128, HA * NB])
    ones_col = din("ones_col", [128, 1], BF16)
    ones_row = din("ones_row", [1, 128], BF16)
    ones_row_f32 = din("ones_row_f32", [1, 128])
    ones128_f32 = din("ones128_f32", [128, 128])
    ident_bf = din("ident_bf", [128, 128], BF16)
    ident_f32 = din("ident_f32", [128, 128])
    eps_col = din("eps_col", [128, 1])
    one_col = din("one_col", [128, 1])
    tri_neg = din("tri_neg", [128, 128])
    tri_negT = din("tri_negT", [128, 128])
    wfold_a = din("wfold_a", [128, NC * HA])
    mmask = din("mmask", [NC, NC])
    jmask8 = din("jmask8", [NC, 1])
    halo_mask = din("halo_mask", [128, NC])

    out_fm = nc.dram_tensor("out_fm", [D, TPC], F32, kind="ExternalOutput").ap()

    idram = {}
    idram_h = {}
    for name, (shape, dt) in DBG_SHAPES.items():
        kind = "ExternalOutput" if name in debug else "Internal"
        h = nc.dram_tensor(name, shape, dt, kind=kind)
        idram_h[name] = h
        idram[name] = h.ap()
    qkv_sp = idram["qkv_sp"]
    gate_a_sp = idram["gate_a_sp"]
    attn0_sp = idram["attn0_sp"]
    attn0_sp_h = idram_h["attn0_sp"]
    qt_sp = idram["qt_sp"]
    qt_sp_h = idram_h["qt_sp"]
    cc_a_src = idram["cc_a_src"]
    x1_d = idram["x1_d"]
    x2_d = idram["x2_d"]
    x3_d = idram["x3_d"]
    xbc_sp = idram["xbc_sp"]
    gate_b_sp = idram["gate_b_sp"]
    y_sp = idram["y_sp"]
    cc_b_src = idram["cc_b_src"]
    cc_b_src_h = idram_h["cc_b_src"]
    cc_a_dst_h = nc.dram_tensor("cc_a_dst", [NC * HA * 128, 128], BF16, kind="Internal")
    cc_a_dst = cc_a_dst_h
    cc_h_src = nc.dram_tensor("cc_h_src", [128, 192], F32, kind="Internal")
    cc_h_dst_h = nc.dram_tensor("cc_h_dst", [NC * 128, 192], F32, kind="Internal")
    cc_h_dst = cc_h_dst_h
    cc_b_dst_h = nc.dram_tensor("cc_b_dst", [NC * 1025, 512], F32, kind="Internal")
    cc_b_dst = cc_b_dst_h
    inpj_w_h = nc_input_handles["inpj_w"]
    dn_w_h = nc_input_handles["dn_w"]

    with nc.Block() as block:
        kb = KB(nc, block)
        ctx = kb.ctx
        qw = kb.sem("qw")
        qx = kb.sem("qx")
        qs = kb.sem("qs")
        qg = kb.sem("qg")
        pe = kb.sem("pe")
        ve = kb.sem("ve")
        ac = kb.sem("ac")
        po = kb.sem("po")
        cc = kb.sem("cc")
        kb.own = {"V": ve, "S": ac}

        def opg(fn):
            kb.wait("G", po, po.n)
            return kb.op("G", fn, inc=po)

        STREAM = ctx.enter_context(nc.sbuf_tensor("STREAM", [128, NKC * TPC], BF16))
        KC = {}
        for nm, ap_, shape, dt in (
            ("ones_col", ones_col, [128, 1], BF16),
            ("ones_row", ones_row, [1, 128], BF16),
            ("ones_row_f", ones_row_f32, [1, 128], F32),
            ("ones128_f", ones128_f32, [128, 128], F32),
            ("ident_bf", ident_bf, [128, 128], BF16),
            ("ident_f", ident_f32, [128, 128], F32),
            ("eps_col", eps_col, [128, 1], F32),
            ("one_col", one_col, [128, 1], F32),
            ("tri_neg", tri_neg, [128, 128], F32),
            ("tri_negT", tri_negT, [128, 128], F32),
            ("norm_a", normw_a, [128, 32], F32),
            ("norm_b", normw_b, [128, 32], F32),
            ("d_bc", d_bc, [128, 64], F32),
            ("bdpow", bdpow, [128, HA * NB], F32),
            ("wfold_a", wfold_a, [128, NC * HA], F32),
            ("convb", convb, [128, 48], F32),
            ("convw", bass.AP(nc_input_handles["convw"], 0,
                              [[4, 128], [512, 48], [1, 4]]), [128, 192], F32),
            ("dtbias", dtbias, [64, 1], F32),
            ("alog", alog, [64, 1], F32),
            ("jmask8", jmask8, [NC, 1], F32),
            ("halo_mask", halo_mask, [128, NC], F32),
            ("mmask", mmask, [NC, NC], F32),
            ("kd_col", kd_col, [128, 2 * HA], F32),
        ):
            t = ctx.enter_context(nc.sbuf_tensor("K_" + nm, shape, dt))
            kb.dma("G", t[tuple(slice(None) for _ in shape)], ap_, qw)
            KC[nm] = t
        consts_tok = qw.n
        K_stats = ctx.enter_context(nc.sbuf_tensor("K_stats", [1, 2 * TPC], F32))

        _lnid = [0]

        def emit_ln_stats(slot, ready_ctr, ready_tok):
            rs = 0
            _lnid[0] += 1
            slot = _lnid[0]
            with contextlib.ExitStack() as _es:
                sq = _es.enter_context(nc.sbuf_tensor(f"sqr{slot}", [128, 2 * TPC], BF16))
                srow = _es.enter_context(nc.sbuf_tensor(f"srow{slot}", [1, 2 * TPC], F32))
                _ = 0
                kb.wait_all("T", ready_ctr)
                kb.wait_all("S", ready_ctr)
                with contextlib.ExitStack() as ps_ctx:
                    sps = [ps_ctx.enter_context(
                        nc.psum_tensor(f"sps{slot}_{i}", [1, 512], F32)) for i in range(4)]
                    for kc in range(NKC):
                        for tsub in range(4):
                            tok = kb.op("T", lambda t, kc=kc, tsub=tsub: t.matmul(
                                sps[tsub][:], KC["ones_col"][:],
                                STREAM[:, kc * TPC + tsub * 512: kc * TPC + (tsub + 1) * 512],
                                start=(kc == 0), stop=(kc == NKC - 1)),
                                inc=pe if (kc == NKC - 1) else None)
                    kb.wait("V", pe, pe.n)
                    for tsub in range(4):
                        kb.op("V", lambda v, tsub=tsub: v.tensor_copy(
                            srow[:, tsub * 512:(tsub + 1) * 512], sps[tsub][:]), inc=ve)
                    sums_done = ve.n
                    kb.flush()
                with contextlib.ExitStack() as ps_ctx:
                    sps = [ps_ctx.enter_context(
                        nc.psum_tensor(f"qps{slot}_{i}", [1, 512], F32)) for i in range(4)]
                    kb.wait("T", ve, sums_done)
                    ring_tok = [0, 0]
                    for kc in range(NKC):
                        slot2 = kc % 2
                        kb.wait("S", pe, ring_tok[slot2])
                        sq_tok = kb.op("S", lambda s, kc=kc, sl=slot2: s.activation(
                            sq[:, sl * TPC:(sl + 1) * TPC],
                            STREAM[:, kc * TPC:(kc + 1) * TPC], AF.Square), inc=ac)
                        kb.wait("T", ac, sq_tok)
                        for tsub in range(4):
                            kb.op("T", lambda t, sl=slot2, tsub=tsub, kc=kc: t.matmul(
                                sps[tsub][:], KC["ones_col"][:],
                                sq[:, sl * TPC + tsub * 512: sl * TPC + (tsub + 1) * 512],
                                start=(kc == 0), stop=(kc == NKC - 1)),
                                inc=pe)
                        ring_tok[slot2] = pe.n
                    kb.wait("V", pe, pe.n)
                    for tsub in range(4):
                        kb.op("V", lambda v, tsub=tsub: v.tensor_copy(
                            srow[:, TPC + tsub * 512:TPC + (tsub + 1) * 512],
                            sps[tsub][:]), inc=ve)
                    kb.flush()
                kb.wait("V", ve, ve.n)
                kb.op("V", lambda v: v.tensor_scalar(
                    srow[:, 0:TPC], srow[:, 0:TPC], 1.0 / D, 0.0, OP.mult, OP.add), inc=ve)
                kb.op("V", lambda v: v.tensor_scalar(
                    srow[:, TPC:], srow[:, TPC:], 1.0 / D, 0.0, OP.mult, OP.add), inc=ve)
                kb.op("V", lambda v: v.tensor_tensor(
                    K_stats[:, rs:rs + TPC], srow[:, 0:TPC], srow[:, 0:TPC], OP.mult), inc=ve)
                kb.op("V", lambda v: v.tensor_tensor(
                    K_stats[:, rs:rs + TPC], srow[:, TPC:], K_stats[:, rs:rs + TPC],
                    OP.subtract), inc=ve)
                tok_v = ve.n
                kb.wait("S", ve, tok_v)
                kb.op("S", lambda s: s.activation(
                    K_stats[:, rs:rs + TPC], K_stats[:, rs:rs + TPC], AF.Ln,
                    bias=KC["eps_col"][0:1, :]), inc=ac)
                kb.op("S", lambda s: s.activation(
                    K_stats[:, rs:rs + TPC], K_stats[:, rs:rs + TPC], AF.Exp,
                    scale=-0.5), inc=ac)
                tok_r = ac.n
                kb.wait("V", ac, tok_r)
                kb.op("V", lambda v: v.tensor_tensor(
                    K_stats[:, rs + TPC:rs + 2 * TPC], srow[:, 0:TPC],
                    K_stats[:, rs:rs + TPC], OP.mult), inc=ve)
                kb.op("V", lambda v: v.tensor_scalar(
                    K_stats[:, rs + TPC:rs + 2 * TPC], K_stats[:, rs + TPC:rs + 2 * TPC],
                    -1.0, 0.0, OP.mult, OP.add), inc=ve)
                kb.flush()
            return ve.n

        def emit_ln_apply(slot, stats_tok):
            rs = 0
            slot = _lnid[0]
            with contextlib.ExitStack() as _es:
                bc = _es.enter_context(nc.sbuf_tensor(f"bc{slot}", [128, 2 * TPC], BF16))
                tmp = _es.enter_context(nc.sbuf_tensor(f"tmpa{slot}", [128, TPC], BF16))
                bcps = _es.enter_context(nc.psum_tensor(f"bcps{slot}", [128, 512], F32))
                _ = 0
                kb.wait("T", ve, stats_tok)
                for r in range(2):
                    for tsub in range(4):
                        mt = kb.op("T", lambda t, r=r, tsub=tsub: t.matmul(
                            bcps[:], KC["ones_row_f"][:],
                            K_stats[:, rs + r * TPC + tsub * 512: rs + r * TPC + (tsub + 1) * 512],
                            start=True, stop=True), inc=pe)
                        kb.wait("V", pe, mt)
                        et = kb.op("V", lambda v, r=r, tsub=tsub: v.tensor_copy(
                            bc[:, r * TPC + tsub * 512: r * TPC + (tsub + 1) * 512],
                            bcps[:]), inc=ve)
                        kb.wait("T", ve, et)
                for kc in range(NKC):
                    kb.op("V", lambda v, kc=kc: v.tensor_tensor(
                        tmp[:], STREAM[:, kc * TPC:(kc + 1) * TPC], bc[:, 0:TPC],
                        OP.mult), inc=ve)
                    kb.op("V", lambda v, kc=kc: v.tensor_tensor(
                        STREAM[:, kc * TPC:(kc + 1) * TPC], tmp[:], bc[:, TPC:],
                        OP.add), inc=ve)
                kb.flush()
            return ve.n

        def emit_proj(w_ap, npair, rhs_ready_tok, evac_fn, name):
            with (
                nc.sbuf_tensor(f"wr_{name}", [128, 2 * 8192], BF16) as wring,
                contextlib.ExitStack() as ps_ctx,
            ):
                mps = [ps_ctx.enter_context(
                    nc.psum_tensor(f"mps_{name}_{i}", [128, 512], F32)) for i in range(4)]
                wsem = [kb.sem(f"qw0_{name}"), kb.sem(f"qw1_{name}")]
                slot_tok = [0, 0]
                bank_free = [0, 0, 0, 0]
                bank_owner = [None] * 4
                kb.wait("T", ve, rhs_ready_tok)
                kb.wait("G", ve, rhs_ready_tok)
                for fp in range(npair):
                    sl = fp % 2
                    kb.wait("G", pe, slot_tok[sl])
                    for kg in range(4):
                        kb.dma("G", wring[:, sl * 8192 + kg * 2048: sl * 8192 + (kg + 1) * 2048],
                               w_ap[fp, kg].rearrange("c k j -> c (k j)"), wsem[sl])
                    kb.wait_all("T", wsem[sl])
                    for f2 in range(2):
                        fc = fp * 2 + f2
                        for tsub in range(4):
                            bk = (fc * 4 + tsub) % 4
                            if bank_owner[bk] is not None:
                                kb.wait("T", bank_owner[bk], bank_free[bk])
                            for kc in range(NKC):
                                kb.op("T", lambda t, sl=sl, kc=kc, f2=f2, tsub=tsub, bk=bk: t.matmul(
                                    mps[bk][:],
                                    wring[:, sl * 8192 + (kc % 8) * 256 + (kc // 8) * 2048
                                          + f2 * 128: sl * 8192 + (kc % 8) * 256
                                          + (kc // 8) * 2048 + f2 * 128 + 128],
                                    STREAM[:, kc * TPC + tsub * 512: kc * TPC + (tsub + 1) * 512],
                                    start=(kc == 0), stop=(kc == NKC - 1)),
                                    inc=pe if kc == NKC - 1 else None)
                            mm_tok = pe.n
                            ectr, etok = evac_fn(fc, tsub, mps[bk], mm_tok)
                            bank_owner[bk] = ectr
                            bank_free[bk] = etok
                    slot_tok[sl] = pe.n
                kb.flush()

        for kc in range(NKC):
            kb.dma("G", STREAM[:, kc * TPC:(kc + 1) * TPC], x0[kc * 128:(kc + 1) * 128, :], qw)
        x0_tok = qw.n
        st_tok = emit_ln_stats(0, qw, x0_tok)
        ln1_tok = emit_ln_apply(0, st_tok)

        with nc.sbuf_tensor("ev_a1", [128, 4 * 512], BF16) as etile:
            ering = [(None, 0)] * 4

            def evac_a1(fc, tsub, ps, mm_tok):
                ei = (fc * 4 + tsub) % 4
                own, tok = ering[ei]
                if own is not None:
                    kb.wait_all("S", own)
                kb.wait("S", pe, mm_tok)
                func = AF.Silu if fc < 96 else AF.Sigmoid
                et = kb.op("S", lambda s, ei=ei, ps=ps, func=func: s.activation(
                    etile[:, ei * 512:(ei + 1) * 512], ps[:], func), inc=ac)
                kb.wait("Y", ac, et)
                dst = (qkv_sp[fc, :, tsub * 512:(tsub + 1) * 512] if fc < 96
                       else gate_a_sp[fc - 96, :, tsub * 512:(tsub + 1) * 512])
                stok = kb.dma("Y", dst, etile[:, ei * 512:(ei + 1) * 512], qs)
                ering[ei] = (qs, stok)
                return ac, et

            emit_proj(qkvg_w, 64, ln1_tok, evac_a1, "a1")
        kb.barrier()

        PH = ["A1", "A2", "B", "C", "D1", "D2", "E", "F"]
        LAST = PH.index(upto)

        def APX(handle, off, pat):
            return bass.AP(handle, off, pat)

        PT = {}

        def phase_a2():
            PT["sinit_es"] = contextlib.ExitStack()
            PT["sinit_a"] = PT["sinit_es"].enter_context(
                nc.sbuf_tensor("sinit_a", [128, HA * 128], BF16))
            sinit_a = PT["sinit_a"]
            qh = kb.sem("qh")
            bd = _attn_consts()[3]
            qkvt = STREAM[:, 0:6144]
            qtt = STREAM[:, 6144:8192]
            qdb = STREAM[:, 8192:10240]
            scm = STREAM[:, 10240:10752]
            vtm = STREAM[:, 10752:11008]
            ktm = STREAM[:, 11008:11264]
            kvt = STREAM[:, 11264:11520]
            a0t = STREAM[:, 11520:11776]
            gath = STREAM[:, 12288:16384]
            with contextlib.ExitStack() as _es:
                ddt = _es.enter_context(nc.sbuf_tensor("a2_dd", [128, 512], F32))
                qdr = _es.enter_context(nc.sbuf_tensor("a2_qdr", [1, TPC], BF16))
                scps = _es.enter_context(nc.psum_tensor("a2_scps", [128, 512], F32))
                tps = _es.enter_context(nc.psum_tensor("a2_tps", [128, 512], BF16))
                ops_ = _es.enter_context(nc.psum_tensor("a2_ops", [128, 256], F32))
                ups = _es.enter_context(nc.psum_tensor("a2_ups", [128, 128], F32))
                bcps = _es.enter_context(nc.psum_tensor("a2_bcps", [128, 2048], F32))
                cons_t = 0
                cons_v = 0
                for h in range(HA):
                    kb.wait("Y", pe, cons_t)
                    kb.wait("Y", ve, cons_v)
                    kb.wait_all("Y", qs)
                    kb.wait_all("V", qs)
                    for i in range(3):
                        kb.dma("Y", qkvt[:, i * TPC:(i + 1) * TPC], qkv_sp[3 * h + i], qh)
                    for mc in range(2):
                        kb.dma("Y", ddt[:, mc * 256:(mc + 1) * 256],
                               ddT[h, mc * 128:(mc + 1) * 128, :], qh)
                    kb.dma("Y", qdr[:, :], qd_row[h], qh)
                    kb.wait_all("T", qh)
                    kb.wait_all("V", qh)
                    for i in range(4):
                        kb.op("T", lambda t, i=i: t.matmul(
                            bcps[:, i * 512:(i + 1) * 512], KC["ones_row"][:],
                            qdr[:, i * 512:(i + 1) * 512],
                            start=True, stop=True), inc=pe)
                    kb.wait("V", pe, pe.n)
                    for i in range(4):
                        kb.op("V", lambda v, i=i: v.tensor_copy(
                            qdb[:, i * 512:(i + 1) * 512], bcps[:, i * 512:(i + 1) * 512]), inc=ve)
                    kb.op("V", lambda v: v.tensor_tensor(
                        qtt[:, :], qkvt[:, 0:TPC], qdb[:, :], OP.mult), inc=ve)
                    qt_tok = ve.n
                    kb.wait("Y", ve, qt_tok)
                    kb.dma("Y", qt_sp[h], qtt[:, :], qs)
                    kb.op("V", lambda v: v.memset(kvt[:, 0:128], 0.0), inc=ve)
                    kv_tok = ve.n
                    for b in range(NB):
                        nb0 = b * BS
                        cur, nxt = (b % 2) * 128, ((b + 1) % 2) * 128
                        for mc in range(2):
                            kb.op("T", lambda t, mc=mc, nb0=nb0: t.matmul(
                                scps[:, mc * 256:(mc + 1) * 256],
                                qkvt[:, TPC + nb0 + mc * 128: TPC + nb0 + (mc + 1) * 128],
                                qkvt[:, nb0: nb0 + 256],
                                start=True, stop=True), inc=pe)
                        for mc in range(2):
                            kb.op("T", lambda t, mc=mc, nb0=nb0: t.transpose(
                                tps[:, mc * 128:(mc + 1) * 128],
                                qkvt[:, 2 * TPC + nb0 + mc * 128: 2 * TPC + nb0 + (mc + 1) * 128],
                                KC["ident_bf"][:]), inc=pe)
                            kb.op("T", lambda t, mc=mc, nb0=nb0: t.transpose(
                                tps[:, 256 + mc * 128: 256 + (mc + 1) * 128],
                                qkvt[:, TPC + nb0 + mc * 128: TPC + nb0 + (mc + 1) * 128],
                                KC["ident_bf"][:]), inc=pe)
                        kb.wait("V", pe, pe.n)
                        for mc in range(2):
                            kb.op("V", lambda v, mc=mc: v.tensor_tensor(
                                scm[:, mc * 256:(mc + 1) * 256],
                                scps[:, mc * 256:(mc + 1) * 256],
                                ddt[:, mc * 256:(mc + 1) * 256], OP.mult), inc=ve)
                            kb.op("V", lambda v, mc=mc: v.tensor_copy(
                                vtm[:, mc * 128:(mc + 1) * 128],
                                tps[:, mc * 128:(mc + 1) * 128]), inc=ve)
                            kb.op("V", lambda v, mc=mc, h=h: v.tensor_scalar(
                                ktm[:, mc * 128:(mc + 1) * 128],
                                tps[:, 256 + mc * 128: 256 + (mc + 1) * 128],
                                KC["kd_col"][:, mc * 32 + h: mc * 32 + h + 1], 0.0,
                                OP.mult, OP.add), inc=ve)
                        prep_tok = ve.n
                        kb.wait("T", ve, prep_tok)
                        for nc2 in range(2):
                            kb.op("T", lambda t, nc2=nc2: t.matmul(
                                ops_[:, nc2 * 128:(nc2 + 1) * 128],
                                scm[:, nc2 * 128: nc2 * 128 + 128],
                                vtm[:, 0:128], start=True, stop=False))
                            kb.op("T", lambda t, nc2=nc2: t.matmul(
                                ops_[:, nc2 * 128:(nc2 + 1) * 128],
                                scm[:, 256 + nc2 * 128: 256 + nc2 * 128 + 128],
                                vtm[:, 128:256], start=False, stop=False))
                            kb.op("T", lambda t, nc2=nc2, nb0=nb0, cur=cur: t.matmul(
                                ops_[:, nc2 * 128:(nc2 + 1) * 128],
                                qtt[:, nb0 + nc2 * 128: nb0 + (nc2 + 1) * 128],
                                kvt[:, cur:cur + 128], start=False, stop=True),
                                inc=pe if nc2 == 1 else None)
                        for mc in range(2):
                            kb.op("T", lambda t, mc=mc: t.matmul(
                                ups[:], ktm[:, mc * 128:(mc + 1) * 128],
                                vtm[:, mc * 128:(mc + 1) * 128],
                                start=(mc == 0), stop=(mc == 1)),
                                inc=pe if mc == 1 else None)
                        kb.wait("V", pe, pe.n)
                        kb.wait_all("V", qs)
                        for nc2 in range(2):
                            et = kb.op("V", lambda v, nc2=nc2: v.tensor_copy(
                                a0t[:, nc2 * 128:(nc2 + 1) * 128],
                                ops_[:, nc2 * 128:(nc2 + 1) * 128]), inc=ve)
                        kb.op("V", lambda v, cur=cur, nxt=nxt, bdh=float(bd[h]): v.scalar_tensor_tensor(
                            kvt[:, nxt:nxt + 128], kvt[:, cur:cur + 128], bdh, ups[:],
                            OP.mult, OP.add), inc=ve)
                        kv_tok = ve.n
                        kb.wait("Y", ve, kv_tok)
                        for nc2 in range(2):
                            kb.dma("Y", attn0_sp[h, 2 * b + nc2],
                                   a0t[:, nc2 * 128:(nc2 + 1) * 128], qs)
                        kb.wait("T", ve, kv_tok)
                    kb.wait("Y", ve, kv_tok)
                    kb.dma("Y", cc_a_src[(h * 128):(h + 1) * 128, :],
                           kvt[:, (NB % 2) * 128:(NB % 2) * 128 + 128], qs)
                    cons_t = pe.n
                    cons_v = ve.n
                kb.flush()
                kb.wait_all("G", qs)
                gtok = kb.op("G", lambda g: g.collective_compute(
                    "AllGather", OP.bypass, replica_groups=[list(range(NC))],
                    ins=[cc_a_src.opt()], outs=[cc_a_dst.ap().opt()]), inc=cc)
                kb.wait("G", cc, gtok)
                kb.op("V", lambda v: v.memset(sinit_a[:, :], 0.0), inc=ve)
                for j in range(NC):
                    kb.wait("G", ve, ve.n)
                    kb.dma("G", gath,
                           APX(cc_a_dst_h, j * HA * 128 * 128,
                               [[128, 128], [128 * 128, HA], [1, 128]]), qg)
                    kb.wait_all("V", qg)
                    for h in range(HA):
                        kb.op("V", lambda v, j=j, h=h: v.scalar_tensor_tensor(
                            sinit_a[:, h * 128:(h + 1) * 128],
                            gath[:, h * 128:(h + 1) * 128],
                            KC["wfold_a"][:, j * HA + h: j * HA + h + 1],
                            sinit_a[:, h * 128:(h + 1) * 128],
                            OP.mult, OP.add), inc=ve)
                kb.flush()

        def phase_b():
            sinit_a = PT["sinit_a"]
            qb = kb.sem("qb")
            with contextlib.ExitStack() as _es:
                a0 = _es.enter_context(nc.sbuf_tensor("b_a0", [128, 4096], BF16))
                qtl = _es.enter_context(nc.sbuf_tensor("b_qt", [128, 4096], BF16))
                sb = _es.enter_context(nc.sbuf_tensor("b_sb", [128, 4096], BF16))
                af = _es.enter_context(nc.sbuf_tensor("b_af", [128, 4096], BF16))
                sq = _es.enter_context(nc.sbuf_tensor("b_sq", [128, 4096], BF16))
                rr = _es.enter_context(nc.sbuf_tensor("b_r", [128, 1], F32))
                crps = _es.enter_context(nc.psum_tensor("b_crps", [128, 2048], F32))
                btp_a = _es.enter_context(nc.psum_tensor("b_tp_a", [128, 512], BF16))
                btp_b = _es.enter_context(nc.psum_tensor("b_tp_b", [128, 512], BF16))
                btp2 = [btp_a, btp_b]
                cons_t = 0
                cons_v = 0
                for tc in range(16):
                    b = tc // 2
                    if tc % 2 == 0:
                        kb.wait("V", pe, cons_t)
                        for h in range(HA):
                            kb.op("V", lambda v, h=h, b=b: v.tensor_scalar(
                                sb[:, h * 128:(h + 1) * 128],
                                sinit_a[:, h * 128:(h + 1) * 128],
                                KC["bdpow"][:, h * NB + b: h * NB + b + 1], 0.0,
                                OP.mult, OP.add), inc=ve)
                    sb_tok = ve.n
                    kb.wait("Y", pe, cons_t)
                    kb.wait("Y", ve, cons_v)
                    for hg in range(4):
                        kb.dma("Y", a0[:, hg * 1024:(hg + 1) * 1024],
                               APX(attn0_sp_h, hg * 8 * 16 * 128 * 128 + tc * 128 * 128,
                                   [[128, 128], [16 * 128 * 128, 8], [1, 128]]), qb)
                        kb.dma("Y", qtl[:, hg * 1024:(hg + 1) * 1024],
                               APX(qt_sp_h, hg * 8 * 128 * TPC + tc * 128,
                                   [[TPC, 128], [128 * TPC, 8], [1, 128]]), qb)
                    kb.wait_all("T", qb)
                    kb.wait("T", ve, sb_tok)
                    kb.wait_all("V", qb)
                    for hg in range(4):
                        if hg >= 2:
                            kb.wait("T", ve, ve.n)
                        for h8 in range(8):
                            h = hg * 8 + h8
                            kb.op("T", lambda t, h=h, hg=hg, h8=h8: t.matmul(
                                crps[:, (hg % 2) * 1024 + h8 * 128: (hg % 2) * 1024 + (h8 + 1) * 128],
                                qtl[:, h * 128:(h + 1) * 128],
                                sb[:, h * 128:(h + 1) * 128],
                                start=True, stop=True), inc=pe)
                        kb.wait("V", pe, pe.n)
                        for h8 in range(8):
                            h = hg * 8 + h8
                            kb.op("V", lambda v, h=h, hg=hg, h8=h8: v.tensor_tensor(
                                af[:, h * 128:(h + 1) * 128],
                                crps[:, (hg % 2) * 1024 + h8 * 128: (hg % 2) * 1024 + (h8 + 1) * 128],
                                a0[:, h * 128:(h + 1) * 128], OP.add), inc=ve)
                    cons_t = pe.n
                    kb.op("V", lambda v: v.tensor_tensor(sq[:, :], af[:, :], af[:, :], OP.mult), inc=ve)
                    kb.op("V", lambda v: v.reduce_sum(rr[:], sq[:, :], axis=AX.X), inc=ve)
                    rtok = ve.n
                    kb.wait("S", ve, rtok)
                    kb.op("S", lambda s: s.activation(rr[:], rr[:], AF.Ln,
                                                      bias=KC["eps_col"][:], scale=1.0 / 4096.0), inc=ac)
                    sa = kb.op("S", lambda s: s.activation(rr[:], rr[:], AF.Exp, scale=-0.5), inc=ac)
                    kb.wait("V", ac, sa)
                    kb.op("V", lambda v: v.tensor_scalar(af[:, :], af[:, :], rr[:], 0.0,
                                                         OP.mult, OP.add), inc=ve)
                    ntok = ve.n
                    cons_v = ve.n
                    kb.wait("T", ve, ntok)
                    for fq in range(8):
                        if fq >= 2:
                            kb.wait("T", ve, ve.n)
                        for f4 in range(4):
                            fc = fq * 4 + f4
                            kb.op("T", lambda t, fc=fc, fq=fq, f4=f4: t.transpose(
                                btp2[fq % 2][:, f4 * 128:(f4 + 1) * 128],
                                af[:, fc * 128:(fc + 1) * 128], KC["ident_bf"][:]), inc=pe)
                        kb.wait("V", pe, pe.n)
                        for f4 in range(4):
                            fc = fq * 4 + f4
                            kb.op("V", lambda v, fc=fc, tc=tc, fq=fq, f4=f4: v.tensor_scalar(
                                STREAM[:, fc * TPC + tc * 128: fc * TPC + (tc + 1) * 128],
                                btp2[fq % 2][:, f4 * 128:(f4 + 1) * 128],
                                KC["norm_a"][:, fc:fc + 1], 0.0, OP.mult, OP.add), inc=ve)
                kb.flush()
            with contextlib.ExitStack() as _es:
                gt = _es.enter_context(nc.sbuf_tensor("b_g", [128, 2 * TPC], BF16))
                gsem = [kb.sem("qbg0"), kb.sem("qbg1")]
                kb.wait_all("Y", qb)
                kb.fence("Y")
                cons = [0, 0]
                for fc in range(NKC):
                    sl = fc % 2
                    kb.wait("Y", ve, cons[sl])
                    kb.dma("Y", gt[:, sl * TPC:(sl + 1) * TPC], gate_a_sp[fc], gsem[sl])
                    kb.wait_all("V", gsem[sl])
                    kb.op("V", lambda v, fc=fc, sl=sl: v.tensor_tensor(
                        STREAM[:, fc * TPC:(fc + 1) * TPC],
                        STREAM[:, fc * TPC:(fc + 1) * TPC],
                        gt[:, sl * TPC:(sl + 1) * TPC], OP.mult), inc=ve)
                    cons[sl] = ve.n
                rhs_tok = ve.n
                kb.flush()
            with contextlib.ExitStack() as _es:
                x0c = _es.enter_context(nc.sbuf_tensor("b_x0", [128, TPC], F32))
                x1t = _es.enter_context(nc.sbuf_tensor("b_x1", [128, 4 * 512], BF16))
                xsem = kb.sem("qbx")
                kb.wait_all("Y", kb.sem("qbg0"))
                kb.wait_all("Y", kb.sem("qbg1"))
                kb.fence("Y")
                st = {"xcons": 0}
                ering = [0, 0, 0, 0]

                def evac_b3(fc, tsub, ps, mm_tok):
                    if tsub == 0:
                        kb.wait("Y", ve, st["xcons"])
                        kb.dma("Y", x0c[:, :], x0[fc * 128:(fc + 1) * 128, :], xsem)
                    ei = (fc * 4 + tsub) % 4
                    kb.wait_all("V", qs)
                    kb.wait_all("V", xsem)
                    kb.wait("V", pe, mm_tok)
                    et = kb.op("V", lambda v, ps=ps, tsub=tsub, ei=ei: v.tensor_tensor(
                        x1t[:, ei * 512:(ei + 1) * 512], ps[:],
                        x0c[:, tsub * 512:(tsub + 1) * 512], OP.add), inc=ve)
                    if tsub == 3:
                        st["xcons"] = ve.n
                    kb.wait("Y", ve, et)
                    kb.dma("Y", x1_d[fc * 128:(fc + 1) * 128, tsub * 512:(tsub + 1) * 512],
                           x1t[:, ei * 512:(ei + 1) * 512], qs)
                    ering[ei] = qs.n
                    return ve, et

                emit_proj(outw_w, 16, rhs_tok, evac_b3, "b3")
                kb.flush()
            PT["sinit_es"].close()

        def phase_ffn(x_in_ap, x_out_ap, name, final=False):
            for kc in range(NKC):
                kb.dma("Y", STREAM[:, kc * TPC:(kc + 1) * TPC],
                       x_in_ap[kc * 128:(kc + 1) * 128, :], qx)
            xtok = qx.n
            st_tok = emit_ln_stats(0, qx, xtok)
            qf = [kb.sem(f"qf0_{name}"), kb.sem(f"qf1_{name}")]
            with contextlib.ExitStack() as _es:
                bc = _es.enter_context(nc.sbuf_tensor(f"f_bc_{name}", [128, 2 * TPC], BF16))
                lnx = _es.enter_context(nc.sbuf_tensor(f"f_lnx_{name}", [128, 2 * TPC], BF16))
                wt = _es.enter_context(nc.sbuf_tensor(f"f_w_{name}", [128, 2 * 4096], BF16))
                zt = _es.enter_context(nc.sbuf_tensor(f"f_z_{name}", [128, 4096], BF16))
                ot = _es.enter_context(nc.sbuf_tensor(f"f_o_{name}", [128, 4 * 512],
                                                      F32 if final else BF16))
                fps = _es.enter_context(nc.psum_tensor(f"f_ps_{name}", [128, 4 * 512], F32))
                fps2 = _es.enter_context(nc.psum_tensor(f"f_ps2_{name}", [128, 2 * 512], F32))
                bcps = _es.enter_context(nc.psum_tensor(f"f_bcps_{name}", [128, 512], F32))
                kb.wait("T", ve, st_tok)
                for r in range(2):
                    for tsub in range(4):
                        mt = kb.op("T", lambda t, r=r, tsub=tsub: t.matmul(
                            bcps[:], KC["ones_row_f"][:],
                            K_stats[:, r * TPC + tsub * 512: r * TPC + (tsub + 1) * 512],
                            start=True, stop=True), inc=pe)
                        kb.wait("V", pe, mt)
                        et = kb.op("V", lambda v, r=r, tsub=tsub: v.tensor_copy(
                            bc[:, r * TPC + tsub * 512: r * TPC + (tsub + 1) * 512],
                            bcps[:]), inc=ve)
                        kb.wait("T", ve, et)
                wcons = [0, 0]
                ocons = [0, 0, 0, 0]
                up_free = [0, 0, 0, 0]
                dn_free = [0, 0]
                kb.wait("G", ve, st_tok)
                kb.fence("G")
                for h in range(16):
                    sl = h % 2
                    kb.wait("G", pe, wcons[sl])
                    for kc2 in range(2):
                        kb.dma("G", wt[:, sl * 4096 + kc2 * 1024: sl * 4096 + (kc2 + 1) * 1024],
                               up_w[h, kc2], qf[sl])
                    kb.dma("G", wt[:, sl * 4096 + 2048: sl * 4096 + 4096],
                           APX(dn_w_h, h * 8 * 128 * 256,
                               [[256, 128], [128 * 256, 8], [1, 256]]), qf[sl])
                    for d2 in range(2):
                        fc = 2 * h + d2
                        kb.op("V", lambda v, fc=fc, d2=d2: v.tensor_tensor(
                            lnx[:, d2 * TPC:(d2 + 1) * TPC],
                            STREAM[:, fc * TPC:(fc + 1) * TPC], bc[:, 0:TPC], OP.mult), inc=ve)
                        kb.op("V", lambda v, d2=d2: v.tensor_tensor(
                            lnx[:, d2 * TPC:(d2 + 1) * TPC],
                            lnx[:, d2 * TPC:(d2 + 1) * TPC], bc[:, TPC:], OP.add), inc=ve)
                    ln_tok = ve.n
                    kb.wait("T", ve, ln_tok)
                    kb.wait_all("T", qf[sl])
                    for tsub in range(4):
                        for m in range(8):
                            pslot = m % 4
                            if up_free[pslot]:
                                kb.wait("T", ac, up_free[pslot])
                            for kc2 in range(2):
                                kb.op("T", lambda t, sl=sl, m=m, kc2=kc2, tsub=tsub, pslot=pslot: t.matmul(
                                    fps[:, pslot * 512:(pslot + 1) * 512],
                                    wt[:, sl * 4096 + kc2 * 1024 + m * 128: sl * 4096 + kc2 * 1024 + m * 128 + 128],
                                    lnx[:, kc2 * TPC + tsub * 512: kc2 * TPC + (tsub + 1) * 512],
                                    start=(kc2 == 0), stop=(kc2 == 1)),
                                    inc=pe if kc2 == 1 else None)
                            mt = pe.n
                            kb.wait("S", pe, mt)
                            up_free[pslot] = kb.op("S", lambda s, m=m, pslot=pslot: s.activation(
                                zt[:, m * 512:(m + 1) * 512],
                                fps[:, pslot * 512:(pslot + 1) * 512], AF.Gelu), inc=ac)
                        z_tok = ac.n
                        kb.wait("T", ac, z_tok)
                        for d2 in range(2):
                            fc = 2 * h + d2
                            pslot = d2 % 2
                            if dn_free[pslot]:
                                kb.wait("T", ve, dn_free[pslot])
                            for mk in range(8):
                                kb.op("T", lambda t, sl=sl, d2=d2, mk=mk, pslot=pslot: t.matmul(
                                    fps2[:, pslot * 512:(pslot + 1) * 512],
                                    wt[:, sl * 4096 + 2048 + mk * 256 + d2 * 128: sl * 4096 + 2048 + mk * 256 + d2 * 128 + 128],
                                    zt[:, mk * 512:(mk + 1) * 512],
                                    start=(mk == 0), stop=(mk == 7)),
                                    inc=pe if mk == 7 else None)
                            mt = pe.n
                            ei = (fc * 4 + tsub) % 4
                            kb.wait_all("V", qs)
                            kb.wait("V", pe, mt)
                            et = kb.op("V", lambda v, fc=fc, tsub=tsub, pslot=pslot, ei=ei: v.tensor_tensor(
                                ot[:, ei * 512:(ei + 1) * 512],
                                fps2[:, pslot * 512:(pslot + 1) * 512],
                                STREAM[:, fc * TPC + tsub * 512: fc * TPC + (tsub + 1) * 512],
                                OP.add), inc=ve)
                            dn_free[pslot] = et
                            kb.wait("Y", ve, et)
                            kb.dma("Y", x_out_ap[fc * 128:(fc + 1) * 128, tsub * 512:(tsub + 1) * 512],
                                   ot[:, ei * 512:(ei + 1) * 512], qs)
                            ocons[ei] = qs.n
                    wcons[sl] = pe.n
                kb.flush()

        def phase_d1():
            PT["mamba_es"] = contextlib.ExitStack()
            PT["dtp"] = PT["mamba_es"].enter_context(nc.sbuf_tensor("dtp_t", [64, TPC], F32))
            PT["cum"] = PT["mamba_es"].enter_context(nc.sbuf_tensor("cum_t", [64, TPC], F32))
            PT["hf"] = PT["mamba_es"].enter_context(nc.sbuf_tensor("d1_hf", [128, 192], F32))
            dtp_t, cum_t, hf = PT["dtp"], PT["cum"], PT["hf"]
            for kc in range(NKC):
                kb.dma("Y", STREAM[:, kc * TPC:(kc + 1) * TPC],
                       x2_d[kc * 128:(kc + 1) * 128, :], qx)
            xtok = qx.n
            st_tok = emit_ln_stats(0, qx, xtok)
            ln3_tok = emit_ln_apply(0, st_tok)
            with contextlib.ExitStack() as _es:
                etile = _es.enter_context(nc.sbuf_tensor("d1_ev", [128, 4 * 512], BF16))
                hst = _es.enter_context(nc.sbuf_tensor("d1_hst", [128, 192], F32))
                kb.op("V", lambda v: v.memset(hst[:, :], 0.0), inc=ve)
                ering = [0, 0, 0, 0]

                def evac_d1(fc, tsub, ps, mm_tok):
                    ei = (fc * 4 + tsub) % 4
                    if fc < 32:
                        kb.wait_all("S", qs)
                        kb.wait("S", pe, mm_tok)
                        et = kb.op("S", lambda s, ei=ei, ps=ps: s.activation(
                            etile[:, ei * 512:(ei + 1) * 512], ps[:], AF.Silu), inc=ac)
                        kb.wait("Y", ac, et)
                        kb.dma("Y", gate_b_sp[fc, :, tsub * 512:(tsub + 1) * 512],
                               etile[:, ei * 512:(ei + 1) * 512], qs)
                        ering[ei] = qs.n
                        return ac, et
                    fi = fc - 32
                    kb.wait_all("V", qs)
                    kb.wait("V", pe, mm_tok)
                    et = kb.op("V", lambda v, ei=ei, ps=ps: v.tensor_copy(
                        etile[:, ei * 512:(ei + 1) * 512], ps[:]), inc=ve)
                    if tsub == 3:
                        et = kb.op("V", lambda v, ei=ei, fi=fi: v.tensor_copy(
                            hst[:, fi * 4 + 1: fi * 4 + 4],
                            etile[:, ei * 512 + 509: ei * 512 + 512]), inc=ve)
                    kb.wait("Y", ve, et)
                    kb.dma("Y", xbc_sp[fi, :, tsub * 512:(tsub + 1) * 512],
                           etile[:, ei * 512:(ei + 1) * 512], qs)
                    ering[ei] = qs.n
                    return ve, et

                emit_proj(inpj_w, 40, ln3_tok, evac_d1, "d1")
                kb.wait("Y", ve, ve.n)
                kb.dma("Y", cc_h_src.ap(), hst[:, :], qs)
                kb.wait_all("G", qs)
                kb.op("G", lambda g: g.collective_compute(
                    "AllGather", OP.bypass, replica_groups=[list(range(NC))],
                    ins=[cc_h_src.ap().opt()], outs=[cc_h_dst.ap().opt()]), inc=cc)
                kb.flush()
            with contextlib.ExitStack() as _es:
                gh = _es.enter_context(nc.sbuf_tensor("d1_gh", [128, NC * 192], F32))
                kb.wait_all("G", kb.sem("qw0_d1"))
                kb.wait_all("G", kb.sem("qw1_d1"))
                kb.fence("G")
                kb.wait("G", cc, cc.n)
                kb.dma("G", gh[:, :],
                       APX(cc_h_dst_h, 0, [[192, 128], [128 * 192, NC], [1, 192]]), qg)
                kb.op("V", lambda v: v.memset(hf[:, :], 0.0), inc=ve)
                kb.wait_all("V", qg)
                for j in range(NC):
                    kb.op("V", lambda v, j=j: v.scalar_tensor_tensor(
                        hf[:, :], gh[:, j * 192:(j + 1) * 192],
                        KC["halo_mask"][:, j:j + 1], hf[:, :], OP.mult, OP.add), inc=ve)
                kb.flush()
            with contextlib.ExitStack() as _es:
                dtw = _es.enter_context(nc.sbuf_tensor("d1_dtw", [128, 4 * 512], BF16))
                dtr = _es.enter_context(nc.sbuf_tensor("d1_dtr", [64, TPC], F32))
                du = _es.enter_context(nc.sbuf_tensor("d1_u", [64, 2 * TPC], F32))
                acol = _es.enter_context(nc.sbuf_tensor("d1_acol", [64, 1], F32))
                zrow = _es.enter_context(nc.sbuf_tensor("d1_zrow", [64, TPC], F32))
                lgrow = _es.enter_context(nc.sbuf_tensor("d1_lgrow", [1, 512], F32))
                dps = _es.enter_context(nc.psum_tensor("d1_dps", [64, 2 * 512], F32))
                lps = _es.enter_context(nc.psum_tensor("d1_lps", [1, 64], F32))
                dsem = kb.sem("qdt")
                kb.wait_all("G", qg)
                kb.fence("G")
                for kg in range(4):
                    kb.dma("G", dtw[:, kg * 512:(kg + 1) * 512],
                           dtpj_w[kg].rearrange("c k j -> c (k j)"), dsem)
                kb.wait_all("T", dsem)
                for tsub in range(4):
                    pslot = tsub % 2
                    for kc in range(NKC):
                        kb.op("T", lambda t, kc=kc, tsub=tsub, pslot=pslot: t.matmul(
                            dps[:, pslot * 512:(pslot + 1) * 512],
                            dtw[:, (kc // 8) * 512 + (kc % 8) * 64: (kc // 8) * 512 + (kc % 8) * 64 + 64],
                            STREAM[:, kc * TPC + tsub * 512: kc * TPC + (tsub + 1) * 512],
                            start=(kc == 0), stop=(kc == NKC - 1)),
                            inc=pe if kc == NKC - 1 else None)
                    mt = pe.n
                    kb.wait("V", pe, mt)
                    et = kb.op("V", lambda v, tsub=tsub, pslot=pslot: v.tensor_copy(
                        dtr[:, tsub * 512:(tsub + 1) * 512],
                        dps[:, pslot * 512:(pslot + 1) * 512]), inc=ve)
                    kb.wait("T", ve, et)
                kb.op("V", lambda v: v.tensor_scalar(
                    du[:, 0:TPC], dtr[:, :], KC["dtbias"][:], 0.0, OP.add, OP.add), inc=ve)
                utok = ve.n
                kb.wait("S", ve, utok)
                kb.op("S", lambda s: s.activation(du[:, TPC:], du[:, 0:TPC], AF.Abs), inc=ac)
                kb.op("S", lambda s: s.activation(du[:, TPC:], du[:, TPC:], AF.Exp, scale=-1.0), inc=ac)
                kb.op("S", lambda s: s.activation(du[:, TPC:], du[:, TPC:], AF.Ln,
                                                  bias=KC["one_col"][0:64, :]), inc=ac)
                kb.op("S", lambda s: s.activation(du[:, 0:TPC], du[:, 0:TPC], AF.Relu), inc=ac)
                stok = ac.n
                kb.wait("V", ac, stok)
                kb.op("V", lambda v: v.tensor_tensor(
                    dtp_t[:, :], du[:, 0:TPC], du[:, TPC:], OP.add), inc=ve)
                dtok = ve.n
                kb.wait("S", ve, dtok)
                atok = kb.op("S", lambda s: s.activation(acol[:], KC["alog"][:], AF.Exp), inc=ac)
                kb.wait("V", ac, atok)
                kb.op("V", lambda v: v.tensor_scalar(
                    zrow[:, :], dtp_t[:, :], acol[:], -1.0, OP.mult, OP.mult), inc=ve)
                kb.op("V", lambda v: v.memset(du[:, 0:TPC], 0.0), inc=ve)
                kb.op("V", lambda v: v.tensor_tensor_scan(
                    cum_t[:, :], zrow[:, :], du[:, 0:TPC], 0.0, OP.add, OP.add), inc=ve)
                cum_tok = ve.n
                kb.wait("T", ve, cum_tok)
                mt = kb.op("T", lambda t: t.matmul(
                    lps[:], cum_t[:, TPC - 1:TPC], KC["ident_f"][0:64, 0:64],
                    start=True, stop=True), inc=pe)
                kb.wait("V", pe, mt)
                kb.op("V", lambda v: v.memset(lgrow[:, :], 0.0), inc=ve)
                kb.op("V", lambda v: v.tensor_copy(lgrow[:, 0:64], lps[:]), inc=ve)
                kb.wait("Y", ve, ve.n)
                kb.dma("Y", APX(cc_b_src_h, 1024 * 512, [[512, 1], [1, 512]]), lgrow[:, :], qs)
                kb.flush()

        def phase_d2():
            if "dbg_cum" in debug:
                kb.wait("Y", ve, ve.n)
                kb.dma("Y", idram["dbg_cum"][0], PT["cum"][:, :], qs)
                kb.dma("Y", idram["dbg_cum"][1], PT["dtp"][:, :], qs)
            PT["sinitb_es"] = contextlib.ExitStack()
            PT["sinit_b"] = PT["sinitb_es"].enter_context(
                nc.sbuf_tensor("sinit_b", [128, 4096], BF16))
            sinit_b = PT["sinit_b"]
            dtp_t, cum_t, hf = PT["dtp"], PT["cum"], PT["hf"]
            qd2 = kb.sem("qd2")
            xt = STREAM[:, 0:24576]
            yt = STREAM[:, 24576:28672]
            xva = STREAM[:, 28672:32768]
            x2a = STREAM[:, 32768:36864]
            ema = STREAM[:, 36864:45056]
            lma = STREAM[:, 45056:53248]
            t1a = STREAM[:, 53248:57344]
            with contextlib.ExitStack() as _es:
                sst = _es.enter_context(nc.sbuf_tensor("d2_st", [128, 4096], F32))
                sbf = _es.enter_context(nc.sbuf_tensor("d2_sbf", [128, 4096], BF16))
                carry = _es.enter_context(nc.sbuf_tensor("d2_carry", [128, 192], BF16))
                ext = _es.enter_context(nc.sbuf_tensor("d2_ext", [128, 516], BF16))
                cv = _es.enter_context(nc.sbuf_tensor("d2_cv", [128, 512], BF16))
                cols = _es.enter_context(nc.sbuf_tensor("d2_cols", [128, 256], F32))
                btm = _es.enter_context(nc.sbuf_tensor("d2_btm", [128, 128], BF16))
                cbt4 = _es.enter_context(nc.sbuf_tensor("d2_cbt4", [128, 512], F32))
                ncol = _es.enter_context(nc.sbuf_tensor("d2_ncol", [128, 64], F32))
                ec_col = _es.enter_context(nc.sbuf_tensor("d2_ecol", [64, 1], F32))
                brow = _es.enter_context(nc.sbuf_tensor("d2_brow", [1, 64], F32))
                rel = _es.enter_context(nc.sbuf_tensor("d2_rel", [64, 256], F32))
                bcb = _es.enter_context(nc.sbuf_tensor("d2_bcb", [128, 64], F32))
                scps = _es.enter_context(nc.psum_tensor("d2_scps", [128, 8 * 128], F32))
                tpm4a = _es.enter_context(nc.psum_tensor("d2_tpm4a", [128, 512], F32))
                tpm4b = _es.enter_context(nc.psum_tensor("d2_tpm4b", [128, 512], F32))
                tpm42 = [tpm4a, tpm4b]
                ctp = tpm4a[:, 0:64]
                bc64 = tpm4a[:, 64:128]
                rrow = tpm4a[0:1, 448:512]
                ops8 = _es.enter_context(nc.psum_tensor("d2_ops8", [128, 512], F32))
                ips8 = _es.enter_context(nc.psum_tensor("d2_ips8", [128, 512], F32))
                tpb = _es.enter_context(nc.psum_tensor("d2_tpb", [128, 512], BF16))
                stps = _es.enter_context(nc.psum_tensor("d2_stps", [128, 512], F32))
                kb.op("V", lambda v: v.memset(sst[:, :], 0.0), inc=ve)
                kb.op("V", lambda v: v.memset(sbf[:, :], 0.0), inc=ve)
                x2a_free = 0
                kb.wait_all("V", qg)
                for fi in range(48):
                    kb.op("V", lambda v, fi=fi: v.tensor_copy(
                        carry[:, fi * 4 + 1: fi * 4 + 4], hf[:, fi * 4 + 1: fi * 4 + 4]), inc=ve)
                for tt in range(4):
                    kb.wait("Y", pe, pe.n)
                    kb.wait("Y", ve, ve.n)
                    kb.wait("Y", ac, ac.n)
                    kb.wait_all("Y", qs)
                    for fc in range(48):
                        kb.dma("Y", xt[:, fc * 512:(fc + 1) * 512],
                               xbc_sp[fc, :, tt * 512:(tt + 1) * 512], qd2)
                    kb.wait_all("V", qd2)
                    for fi in range(48):
                        kb.op("V", lambda v, fi=fi: v.tensor_copy(
                            ext[:, 1:4], carry[:, fi * 4 + 1: fi * 4 + 4]), inc=ve)
                        kb.op("V", lambda v, fi=fi: v.tensor_copy(
                            ext[:, 4:516], xt[:, fi * 512:(fi + 1) * 512]), inc=ve)
                        kb.op("V", lambda v, fi=fi: v.tensor_copy(
                            carry[:, fi * 4 + 1: fi * 4 + 4], ext[:, 513:516]), inc=ve)
                        kb.op("V", lambda v, fi=fi: v.tensor_scalar(
                            cv[:, :], ext[:, 1:513],
                            KC["convw"][:, fi * 4: fi * 4 + 1], KC["convb"][:, fi:fi + 1],
                            OP.mult, OP.add), inc=ve)
                        for k in range(1, 4):
                            kb.op("V", lambda v, fi=fi, k=k: v.scalar_tensor_tensor(
                                cv[:, :], ext[:, 1 + k:513 + k],
                                KC["convw"][:, fi * 4 + k: fi * 4 + k + 1], cv[:, :],
                                OP.mult, OP.add), inc=ve)
                        cv_tok = ve.n
                        kb.wait("S", ve, cv_tok)
                        st = kb.op("S", lambda s, fi=fi: s.activation(
                            xt[:, fi * 512:(fi + 1) * 512], cv[:, :], AF.Silu), inc=ac)
                        kb.wait("V", ac, st)
                    silu_tok = ac.n
                    kb.wait("T", ac, silu_tok)
                    if tt == 0 and "dbg_xt" in debug:
                        kb.wait("Y", ac, silu_tok)
                        for fc in range(48):
                            kb.dma("Y", idram["dbg_xt"][fc], xt[:, fc * 512:(fc + 1) * 512], qs)
                    for ch in range(4):
                        cg = tt * 4 + ch
                        t0 = cg * MC
                        kb.wait("V", ac, ac.n)
                        kb.wait_all("V", qs)
                        if cg == 0:
                            kb.op("V", lambda v, t0=t0: v.tensor_scalar(
                                rel[:, 0:128], cum_t[:, t0:t0 + 128], 0.0, 0.0,
                                OP.add, OP.add), inc=ve)
                        else:
                            kb.op("V", lambda v, t0=t0: v.tensor_scalar(
                                rel[:, 0:128], cum_t[:, t0:t0 + 128],
                                cum_t[:, t0 - 1:t0], 0.0, OP.subtract, OP.add), inc=ve)
                        kb.op("V", lambda v, t0=t0: v.tensor_scalar(
                            rel[:, 128:256], cum_t[:, t0:t0 + 128],
                            cum_t[:, t0 + 127:t0 + 128], 0.0, OP.subtract, OP.add), inc=ve)
                        rel_tok = ve.n
                        kb.wait("T", ve, rel_tok)
                        kb.wait("T", ac, ac.n)
                        mt = kb.op("T", lambda t: t.transpose(
                            ctp, rel[:, 0:128], KC["ident_f"][0:64, 0:64]), inc=pe)
                        kb.wait("V", pe, mt)
                        kb.op("V", lambda v: v.tensor_copy(cols[:, 0:64], ctp), inc=ve)
                        c1 = ve.n
                        kb.wait("S", ve, c1)
                        bt_tok = kb.op("S", lambda s: s.activation(
                            cols[:, 64:128], cols[:, 0:64], AF.Exp), inc=ac)
                        kb.wait("T", ve, c1)
                        mt = kb.op("T", lambda t: t.transpose(
                            ctp, rel[:, 128:256], KC["ident_f"][0:64, 0:64]), inc=pe)
                        kb.wait("S", pe, mt)
                        vx_tok = kb.op("S", lambda s: s.activation(
                            cols[:, 128:192], ctp, AF.Exp, scale=-1.0), inc=ac)
                        kb.wait("T", ac, vx_tok)
                        mt = kb.op("T", lambda t, t0=t0: t.transpose(
                            ctp, dtp_t[:, t0:t0 + 128], KC["ident_f"][0:64, 0:64]), inc=pe)
                        kb.wait("V", pe, mt)
                        kb.op("V", lambda v: v.tensor_copy(cols[:, 192:256], ctp), inc=ve)
                        kb.wait("V", ac, vx_tok)
                        kb.op("V", lambda v: v.tensor_tensor(
                            cols[:, 128:192], cols[:, 128:192], cols[:, 192:256], OP.mult), inc=ve)
                        kb.op("V", lambda v: v.tensor_scalar(
                            ncol[:, :], cols[:, 0:64], -1.0, 0.0, OP.mult, OP.add), inc=ve)
                        cols_tok = ve.n
                        kb.wait("S", ve, rel_tok)
                        ecx = kb.op("S", lambda s: s.activation(
                            ec_col[:], rel[:, 127:128], AF.Exp), inc=ac)
                        kb.wait("T", ac, ecx)
                        kb.wait("T", ve, cols_tok)
                        mt = kb.op("T", lambda t: t.matmul(
                            rrow, ec_col[:], KC["ident_f"][0:64, 0:64],
                            start=True, stop=True), inc=pe)
                        kb.wait("V", pe, mt)
                        brt = kb.op("V", lambda v: v.tensor_copy(brow[:], rrow), inc=ve)
                        kb.wait("T", ve, brt)
                        mt = kb.op("T", lambda t: t.matmul(
                            bc64, KC["ones_row_f"][:], brow[:],
                            start=True, stop=True), inc=pe)
                        kb.wait("V", pe, mt)
                        kb.op("V", lambda v: v.tensor_copy(bcb[:], bc64), inc=ve)
                        bcb_tok = ve.n
                        kb.wait("T", ve, ve.n)
                        for g in range(MG):
                            kb.op("T", lambda t, g=g, ch=ch: t.matmul(
                                scps[:, g * 128:(g + 1) * 128],
                                xt[:, (32 + g) * 512 + ch * 128: (32 + g) * 512 + (ch + 1) * 128],
                                xt[:, (40 + g) * 512 + ch * 128: (40 + g) * 512 + (ch + 1) * 128],
                                start=True, stop=True), inc=pe)
                        sc_tok = pe.n
                        kb.wait("T", ve, x2a_free)
                        tpb_free = 0
                        for q in range(8):
                            kb.wait("T", ve, tpb_free)
                            for p4 in range(4):
                                pr = q * 4 + p4
                                kb.op("T", lambda t, pr=pr, ch=ch: t.transpose(
                                    tpb[:, (pr % 4) * 128:((pr % 4) + 1) * 128],
                                    xt[:, pr * 512 + ch * 128: pr * 512 + (ch + 1) * 128],
                                    KC["ident_bf"][:]), inc=pe)
                            kb.wait("V", pe, pe.n)
                            for p4 in range(4):
                                pr = q * 4 + p4
                                kb.op("V", lambda v, pr=pr: v.tensor_copy(
                                    x2a[:, pr * 128:(pr + 1) * 128],
                                    tpb[:, (pr % 4) * 128:((pr % 4) + 1) * 128]), inc=ve)
                            tpb_free = ve.n
                        kb.wait("V", ve, cols_tok)
                        tpm_free = [0, 0]
                        cbt_free = 0
                        kb.wait("G", ve, cols_tok)
                        for q in range(16):
                            sl4 = q % 2
                            kb.wait("G", pe, cbt_free)
                            for q4 in range(4):
                                h = q * 4 + q4
                                opg(lambda g, h=h, q4=q4: g.tensor_scalar(
                                    cbt4[:, q4 * 128:(q4 + 1) * 128], KC["tri_negT"][:],
                                    cols[:, h:h + 1], 0.0, OP.add, OP.add))
                            cb_tok = po.n
                            kb.wait("T", po, cb_tok)
                            kb.wait("T", ac, tpm_free[sl4])
                            for q4 in range(4):
                                kb.op("T", lambda t, q4=q4, sl4=sl4: t.transpose(
                                    tpm42[sl4][:, q4 * 128:(q4 + 1) * 128],
                                    cbt4[:, q4 * 128:(q4 + 1) * 128],
                                    KC["ident_f"][:]), inc=pe)
                            cbt_free = pe.n
                            kb.wait("S", pe, pe.n)
                            for q4 in range(4):
                                h = q * 4 + q4
                                kb.op("S", lambda s, h=h, q4=q4, sl4=sl4: s.activation(
                                    ema[:, h * 128:(h + 1) * 128],
                                    tpm42[sl4][:, q4 * 128:(q4 + 1) * 128],
                                    AF.Exp, bias=ncol[:, h:h + 1]), inc=ac)
                            tpm_free[sl4] = ac.n
                        em_tok = ac.n
                        kb.wait("V", ac, em_tok)
                        kb.wait("V", pe, sc_tok)
                        for h in range(MH):
                            g = h // 8
                            kb.op("V", lambda v, h=h, g=g: v.scalar_tensor_tensor(
                                lma[:, h * 128:(h + 1) * 128],
                                scps[:, g * 128:(g + 1) * 128],
                                cols[:, 192 + h:193 + h],
                                ema[:, h * 128:(h + 1) * 128],
                                OP.mult, OP.mult), inc=ve)
                        kb.wait("G", ve, ve.n)
                        kb.wait("G", pe, pe.n)
                        for h in range(MH):
                            xsl = (h // 2) * 128 + (h % 2) * 64
                            opg(lambda g, h=h, xsl=xsl: g.tensor_scalar(
                                xva[:, h * 64:(h + 1) * 64], x2a[:, xsl:xsl + 64],
                                cols[:, 128 + h:129 + h], 0.0, OP.mult, OP.add))
                        xva_tok = po.n
                        app_tok = ve.n
                        kb.wait("T", ve, app_tok)
                        kb.wait("T", po, xva_tok)
                        for g in range(MG):
                            for h8 in range(8):
                                h = g * 8 + h8
                                xsl = (h // 2) * 128 + (h % 2) * 64
                                kb.op("T", lambda t, h=h, h8=h8, xsl=xsl: t.matmul(
                                    ops8[:, h8 * 64:(h8 + 1) * 64],
                                    lma[:, h * 128:(h + 1) * 128],
                                    x2a[:, xsl:xsl + 64],
                                    start=True, stop=True), inc=pe)
                            for h8 in range(8):
                                h = g * 8 + h8
                                kb.op("T", lambda t, g=g, ch=ch, h=h, h8=h8: t.matmul(
                                    ips8[:, h8 * 64:(h8 + 1) * 64],
                                    xt[:, (40 + g) * 512 + ch * 128: (40 + g) * 512 + (ch + 1) * 128],
                                    sbf[:, h * 64:(h + 1) * 64],
                                    start=True, stop=True), inc=pe)
                            kb.op("T", lambda t, g=g, ch=ch: t.transpose(
                                tpb[:, 0:128], xt[:, (32 + g) * 512 + ch * 128: (32 + g) * 512 + (ch + 1) * 128],
                                KC["ident_bf"][:]), inc=pe)
                            mm_tok = pe.n
                            kb.wait("V", pe, mm_tok)
                            for h8 in range(8):
                                h = g * 8 + h8
                                kb.op("V", lambda v, h=h, h8=h8: v.tensor_scalar(
                                    t1a[:, h * 64:(h + 1) * 64], ips8[:, h8 * 64:(h8 + 1) * 64],
                                    cols[:, 64 + h:65 + h], 0.0, OP.mult, OP.add), inc=ve)
                            for h8 in range(8):
                                h = g * 8 + h8
                                kb.op("V", lambda v, h=h, h8=h8: v.tensor_tensor(
                                    t1a[:, h * 64:(h + 1) * 64], t1a[:, h * 64:(h + 1) * 64],
                                    ops8[:, h8 * 64:(h8 + 1) * 64], OP.add), inc=ve)
                            for h8 in range(8):
                                h = g * 8 + h8
                                xsl = (h // 2) * 128 + (h % 2) * 64
                                kb.op("V", lambda v, h=h, xsl=xsl: v.scalar_tensor_tensor(
                                    yt[:, h * 64:(h + 1) * 64], x2a[:, xsl:xsl + 64],
                                    KC["d_bc"][:, h:h + 1], t1a[:, h * 64:(h + 1) * 64],
                                    OP.mult, OP.add), inc=ve)
                            bt2 = kb.op("V", lambda v: v.tensor_copy(btm[:], tpb[:, 0:128]), inc=ve)
                            kb.wait("T", ve, bt2)
                            mt = kb.op("T", lambda t, g=g: t.matmul(
                                stps[:], btm[:], xva[:, g * 512:(g + 1) * 512],
                                start=True, stop=True), inc=pe)
                            kb.wait("V", pe, mt)
                            kb.wait("V", ve, bcb_tok)
                            for h8 in range(8):
                                h = g * 8 + h8
                                kb.op("V", lambda v, h=h, h8=h8: v.scalar_tensor_tensor(
                                    sst[:, h * 64:(h + 1) * 64], sst[:, h * 64:(h + 1) * 64],
                                    bcb[:, h:h + 1], stps[:, h8 * 64:(h8 + 1) * 64],
                                    OP.mult, OP.add), inc=ve)
                            kb.wait("T", ve, ve.n)
                        x2a_free = ve.n
                        kb.op("V", lambda v: v.tensor_copy(sbf[:, :], sst[:, :]), inc=ve)
                        y_tok = ve.n
                        kb.wait("Y", ve, y_tok)
                        kb.dma("Y", y_sp[cg], yt, qs)
                kb.flush()
                kb.wait("Y", ve, ve.n)
                for g in range(MG):
                    kb.dma("Y", APX(cc_b_src_h, g * 128 * 512, [[512, 128], [1, 512]]),
                           sst[:, g * 512:(g + 1) * 512], qs)
                kb.flush()
            with contextlib.ExitStack() as _es:
                lg8 = _es.enter_context(nc.sbuf_tensor("d2_lg8", [NC, 64], F32))
                w8 = _es.enter_context(nc.sbuf_tensor("d2_w8", [NC, 64], F32))
                w8t = _es.enter_context(nc.sbuf_tensor("d2_w8t", [64, NC], F32))
                wrow = _es.enter_context(nc.sbuf_tensor("d2_wrow", [1, 64], F32))
                wb = _es.enter_context(nc.sbuf_tensor("d2_wb", [128, 64], F32))
                gth = _es.enter_context(nc.sbuf_tensor("d2_gth", [128, 4096], BF16))
                wps = _es.enter_context(nc.psum_tensor("d2_wps", [NC, 64], F32))
                wtp = _es.enter_context(nc.psum_tensor("d2_wtp", [64, NC], F32))
                rps = _es.enter_context(nc.psum_tensor("d2_rps", [1, 64], F32))
                bps = _es.enter_context(nc.psum_tensor("d2_bps", [128, 64], F32))
                kb.wait_all("G", qs)
                kb.wait_all("G", kb.sem("qd2"))
                kb.fence("G")
                gtok = kb.op("G", lambda g: g.collective_compute(
                    "AllGather", OP.bypass, replica_groups=[list(range(NC))],
                    ins=[cc_b_src.opt()], outs=[cc_b_dst.ap().opt()]), inc=cc)
                kb.wait("G", cc, gtok)
                kb.dma("G", lg8[:, :],
                       APX(cc_b_dst_h, 1024 * 512, [[1025 * 512, NC], [1, 64]]), qg)
                kb.wait_all("T", qg)
                mt = kb.op("T", lambda t: t.matmul(
                    wps[:], KC["mmask"][:, :], lg8[:, :], start=True, stop=True), inc=pe)
                kb.wait("S", pe, mt)
                wtok = kb.op("S", lambda s: s.activation(w8[:, :], wps[:], AF.Exp), inc=ac)
                kb.wait("V", ac, wtok)
                jt = kb.op("V", lambda v: v.tensor_scalar(
                    w8[:, :], w8[:, :], KC["jmask8"][:, :], 0.0, OP.mult, OP.add), inc=ve)
                kb.wait("T", ve, jt)
                mt = kb.op("T", lambda t: t.transpose(
                    wtp[:], w8[:, :], KC["ident_f"][0:NC, 0:NC]), inc=pe)
                kb.wait("V", pe, mt)
                w8t_tok = kb.op("V", lambda v: v.tensor_copy(w8t[:, :], wtp[:]), inc=ve)
                kb.op("V", lambda v: v.memset(sinit_b[:, :], 0.0), inc=ve)
                kb.wait("T", ve, w8t_tok)
                for j in range(NC):
                    mt = kb.op("T", lambda t, j=j: t.matmul(
                        rps[:], w8t[:, j:j + 1], KC["ident_f"][0:64, 0:64],
                        start=True, stop=True), inc=pe)
                    kb.wait("V", pe, mt)
                    rt = kb.op("V", lambda v: v.tensor_copy(wrow[:, :], rps[:]), inc=ve)
                    kb.wait("T", ve, rt)
                    mt = kb.op("T", lambda t: t.matmul(
                        bps[:], KC["ones_row_f"][:], wrow[:, :], start=True, stop=True), inc=pe)
                    kb.wait("V", pe, mt)
                    kb.op("V", lambda v: v.tensor_copy(wb[:, :], bps[:]), inc=ve)
                    kb.wait("G", ve, ve.n)
                    kb.dma("G", gth[:, :],
                           APX(cc_b_dst_h, j * 1025 * 512,
                               [[512, 128], [128 * 512, 8], [1, 512]]), qg)
                    kb.wait_all("V", qg)
                    for h in range(MH):
                        kb.op("V", lambda v, h=h: v.scalar_tensor_tensor(
                            sinit_b[:, h * 64:(h + 1) * 64], gth[:, h * 64:(h + 1) * 64],
                            wb[:, h:h + 1], sinit_b[:, h * 64:(h + 1) * 64],
                            OP.mult, OP.add), inc=ve)
                kb.flush()

        def phase_e():
            sinit_b = PT["sinit_b"]
            dtp_t, cum_t = PT["dtp"], PT["cum"]
            qe = kb.sem("qe")
            qy = kb.sem("qy")
            with contextlib.ExitStack() as _es:
                ecum = _es.enter_context(nc.sbuf_tensor("e_ec", [64, TPC], F32))
                ectm = _es.enter_context(nc.sbuf_tensor("e_ecT", [128, 16 * 64], F32))
                ct = _es.enter_context(nc.sbuf_tensor("e_ct", [128, 8 * 512], BF16))
                ytl = _es.enter_context(nc.sbuf_tensor("e_y", [128, 4096], BF16))
                yf = ytl
                sq = _es.enter_context(nc.sbuf_tensor("e_sq", [128, 512], BF16))
                rr8 = _es.enter_context(nc.sbuf_tensor("e_r", [128, 8], F32))
                cps = _es.enter_context(nc.psum_tensor("e_cps", [128, 2 * 512], F32))
                etp = _es.enter_context(nc.psum_tensor("e_tp", [128, 64], F32))
                tpb_a = _es.enter_context(nc.psum_tensor("e_tpb_a", [128, 512], BF16))
                tpb_b = _es.enter_context(nc.psum_tensor("e_tpb_b", [128, 512], BF16))
                tpb2 = [tpb_a, tpb_b]
                kb.wait("S", ve, ve.n)
                ec_tok = kb.op("S", lambda s: s.activation(ecum[:, :], cum_t[:, :], AF.Exp), inc=ac)
                kb.wait("T", ac, ec_tok)
                for tc in range(16):
                    mt = kb.op("T", lambda t, tc=tc: t.transpose(
                        etp[:], ecum[:, tc * 128:(tc + 1) * 128], KC["ident_f"][0:64, 0:64]), inc=pe)
                    kb.wait("V", pe, mt)
                    et = kb.op("V", lambda v, tc=tc: v.tensor_copy(
                        ectm[:, tc * 64:(tc + 1) * 64], etp[:]), inc=ve)
                    kb.wait("T", ve, et)
                ycons = 0
                ycons_pe = 0
                for tt in range(4):
                    kb.wait("Y", pe, pe.n)
                    for g in range(8):
                        kb.dma("Y", ct[:, g * 512:(g + 1) * 512],
                               xbc_sp[40 + g, :, tt * 512:(tt + 1) * 512], qe)
                    for ch in range(4):
                        cg = tt * 4 + ch
                        kb.wait("Y", ve, ycons)
                        kb.wait("Y", pe, ycons_pe)
                        kb.dma("Y", ytl[:, :], y_sp[cg], qy)
                        kb.wait_all("T", qe)
                        kb.wait_all("V", qy)
                        for g in range(MG):
                            mt = kb.op("T", lambda t, g=g, ch=ch: t.matmul(
                                cps[:, (g % 2) * 512:(g % 2) * 512 + 512],
                                ct[:, g * 512 + ch * 128: g * 512 + (ch + 1) * 128],
                                sinit_b[:, g * 512:(g + 1) * 512],
                                start=True, stop=True), inc=pe)
                            kb.wait("V", pe, mt)
                            for h8 in range(8):
                                h = g * 8 + h8
                                kb.op("V", lambda v, g=g, h=h, h8=h8, cg=cg: v.scalar_tensor_tensor(
                                    yf[:, h * 64:(h + 1) * 64],
                                    cps[:, (g % 2) * 512 + h8 * 64: (g % 2) * 512 + (h8 + 1) * 64],
                                    ectm[:, cg * 64 + h: cg * 64 + h + 1],
                                    ytl[:, h * 64:(h + 1) * 64],
                                    OP.mult, OP.add), inc=ve)
                            kb.wait("T", ve, ve.n)
                        ycons = ve.n
                        for g in range(MG):
                            kb.op("V", lambda v, g=g: v.tensor_tensor(
                                sq[:, :], yf[:, g * 512:(g + 1) * 512],
                                yf[:, g * 512:(g + 1) * 512], OP.mult), inc=ve)
                            kb.op("V", lambda v, g=g: v.reduce_sum(
                                rr8[:, g:g + 1], sq[:, :], axis=AX.X), inc=ve)
                        rtok = ve.n
                        kb.wait("S", ve, rtok)
                        kb.op("S", lambda s: s.activation(rr8[:, :], rr8[:, :], AF.Ln,
                                                          bias=KC["eps_col"][:], scale=1.0 / 512.0), inc=ac)
                        sa = kb.op("S", lambda s: s.activation(rr8[:, :], rr8[:, :], AF.Exp,
                                                               scale=-0.5), inc=ac)
                        kb.wait("V", ac, sa)
                        for g in range(MG):
                            kb.op("V", lambda v, g=g: v.tensor_scalar(
                                yf[:, g * 512:(g + 1) * 512], yf[:, g * 512:(g + 1) * 512],
                                rr8[:, g:g + 1], 0.0, OP.mult, OP.add), inc=ve)
                        yn_tok = ve.n
                        kb.wait("T", ve, yn_tok)
                        for fq in range(8):
                            if fq >= 2:
                                kb.wait("T", ve, ve.n)
                            for f4 in range(4):
                                fc = fq * 4 + f4
                                kb.op("T", lambda t, fc=fc, fq=fq, f4=f4: t.transpose(
                                    tpb2[fq % 2][:, f4 * 128:(f4 + 1) * 128],
                                    yf[:, fc * 128:(fc + 1) * 128], KC["ident_bf"][:]), inc=pe)
                            kb.wait("V", pe, pe.n)
                            for f4 in range(4):
                                fc = fq * 4 + f4
                                kb.op("V", lambda v, fc=fc, cg=cg, fq=fq, f4=f4: v.tensor_scalar(
                                    STREAM[:, fc * TPC + cg * 128: fc * TPC + (cg + 1) * 128],
                                    tpb2[fq % 2][:, f4 * 128:(f4 + 1) * 128],
                                    KC["norm_b"][:, fc:fc + 1], 0.0, OP.mult, OP.add), inc=ve)
                        ycons = ve.n
                        ycons_pe = pe.n
                kb.flush()
            with contextlib.ExitStack() as _es:
                gt = _es.enter_context(nc.sbuf_tensor("e_g", [128, 2 * TPC], BF16))
                gsem = [kb.sem("qeg0"), kb.sem("qeg1")]
                kb.wait_all("Y", qe)
                kb.wait_all("Y", qy)
                kb.fence("Y")
                cons = [0, 0]
                for fc in range(NKC):
                    sl = fc % 2
                    kb.wait("Y", ve, cons[sl])
                    kb.dma("Y", gt[:, sl * TPC:(sl + 1) * TPC], gate_b_sp[fc], gsem[sl])
                    kb.wait_all("V", gsem[sl])
                    kb.op("V", lambda v, fc=fc, sl=sl: v.tensor_tensor(
                        STREAM[:, fc * TPC:(fc + 1) * TPC],
                        STREAM[:, fc * TPC:(fc + 1) * TPC],
                        gt[:, sl * TPC:(sl + 1) * TPC], OP.mult), inc=ve)
                    cons[sl] = ve.n
                rhs_tok = ve.n
                kb.flush()
            PT["sinitb_es"].close()
            PT["mamba_es"].close()
            with contextlib.ExitStack() as _es:
                x2c = _es.enter_context(nc.sbuf_tensor("e_x2", [128, TPC], BF16))
                x3t = _es.enter_context(nc.sbuf_tensor("e_x3", [128, 4 * 512], BF16))
                xsem = kb.sem("qex")
                kb.wait_all("Y", kb.sem("qeg0"))
                kb.wait_all("Y", kb.sem("qeg1"))
                kb.fence("Y")
                st = {"xcons": 0}
                ering = [0, 0, 0, 0]

                def evac_e3(fc, tsub, ps, mm_tok):
                    if tsub == 0:
                        kb.wait("Y", ve, st["xcons"])
                        kb.dma("Y", x2c[:, :], x2_d[fc * 128:(fc + 1) * 128, :], xsem)
                    ei = (fc * 4 + tsub) % 4
                    kb.wait_all("V", qs)
                    kb.wait_all("V", xsem)
                    kb.wait("V", pe, mm_tok)
                    et = kb.op("V", lambda v, ps=ps, tsub=tsub, ei=ei: v.tensor_tensor(
                        x3t[:, ei * 512:(ei + 1) * 512], ps[:],
                        x2c[:, tsub * 512:(tsub + 1) * 512], OP.add), inc=ve)
                    if tsub == 3:
                        st["xcons"] = ve.n
                    kb.wait("Y", ve, et)
                    kb.dma("Y", x3_d[fc * 128:(fc + 1) * 128, tsub * 512:(tsub + 1) * 512],
                           x3t[:, ei * 512:(ei + 1) * 512], qs)
                    ering[ei] = qs.n
                    return ve, et

                emit_proj(outp_w, 16, rhs_tok, evac_e3, "e3")
                kb.flush()

        if LAST >= 1:
            phase_a2()
            kb.barrier()
        if LAST >= 2:
            phase_b()
            kb.barrier()
        if LAST >= 3:
            phase_ffn(x1_d, x2_d, "c")
            kb.barrier()
        if LAST >= 4:
            phase_d1()
            kb.barrier()
        if LAST >= 5:
            phase_d2()
            kb.barrier()
        if LAST >= 6:
            phase_e()
            kb.barrier()
        if LAST >= 7:
            phase_ffn(x3_d, out_fm, "f", final=True)
            kb.barrier()

        if upto != "F":
            with nc.sbuf_tensor("zz", [128, 512], F32) as zz:
                kb.op("V", lambda v: v.memset(zz[:], 0.0), inc=ve)
                kb.wait("Y", ve, ve.n)
                for kc in range(NKC):
                    for tsub in range(4):
                        kb.dma("Y", out_fm[kc * 128:(kc + 1) * 128, tsub * 512:(tsub + 1) * 512],
                               zz[:], qs)
                kb.flush()
        kb.wait("Y", qs, qs.n)
        kb.wait("Y", qw, qw.n)
        kb.flush()
        for _k in ("sinitb_es", "mamba_es", "sinit_es"):
            if _k in PT:
                PT[_k].close()
        kb.ctx.close()
    return nc

def kernel(**inputs):
    in_maps, perm = _host_prep(inputs)
    nc = build_program()
    res = run_bass_kernel_spmd(nc, in_maps, core_ids=list(range(NC)))
    inv = np.argsort(perm)
    full = np.concatenate([np.asarray(r["out_fm"]).T for r in res.results], axis=0)
    return full[:, inv][None].astype(np.float32)


# version 06 (final)
# speedup vs baseline: 1.2091x; 1.2091x over previous
import contextlib
import numpy as np
import ml_dtypes
import concourse.bass as bass
import concourse.mybir as mybir
from concourse.bass_utils import run_bass_kernel_spmd

F32 = mybir.dt.float32
BF16 = mybir.dt.bfloat16
AF = mybir.ActivationFunctionType
OP = mybir.AluOpType
AX = mybir.AxisListType

NC = 8
S = 16384
D = 4096
TPC = S // NC
NKC = D // 128
BS = 256
NB = TPC // BS
HA = 32
MH, MP, MN, MG = 64, 64, 128, 8
MC = 128
NMC = TPC // MC
EPS = 1e-5

_bf = ml_dtypes.bfloat16


def _attn_consts():
    base = 1.0 / 2.0 ** (8.0 / HA)
    expo = np.arange(HA, dtype=np.float64) + 1.0
    factor = 1.0 - 1 / (32 - 1 + 1e-05) + 1e-05
    slope = base ** expo * factor
    r = np.arange(BS, dtype=np.float64) + 1.0
    q_decay = np.exp(-slope[:, None] * r[None, :])
    k_decay = np.exp(-slope[:, None] * (BS - r)[None, :])
    diff = r[:, None] - r[None, :]
    dd = np.where(diff >= 0, np.exp(-slope[:, None, None] * diff[None]), 0.0)
    bd = np.exp(-slope * BS)
    return q_decay, k_decay, dd, bd


def _host_prep(inputs):
    perm = np.asarray(inputs["L0_ffn_perm"]).astype(np.int64)
    qd, kd, dd, bdec = _attn_consts()

    x = np.asarray(inputs["L0_attn_a_hidden_states"])[0]
    xp = np.ascontiguousarray(x[:, perm])

    def blkw(Wt, npair):
        return np.ascontiguousarray(
            Wt.reshape(4, 8, 128, npair, 256).transpose(3, 0, 2, 1, 4))

    qkvT = np.asarray(inputs["L0_attn_a_qkv_w"])[:, perm].T
    gateT = np.asarray(inputs["L0_attn_a_gate_w"])[:, perm].T
    qkvg = blkw(np.concatenate([qkvT, gateT], axis=1), 64)
    outw = blkw(np.asarray(inputs["L0_attn_a_out_w"])[perm, :].T, 16)
    inpT = np.asarray(inputs["L0_attn_b_in_proj_w"])[:, perm].T
    inpw = blkw(np.ascontiguousarray(inpT[:, :10240]), 40)
    dtw = np.ascontiguousarray(
        inpT[:, 10240:].reshape(4, 8, 128, 64).transpose(0, 2, 1, 3))
    outpw = blkw(np.asarray(inputs["L0_attn_b_out_proj_w"])[perm, :].T, 16)
    upw = np.ascontiguousarray(np.asarray(inputs["L0_ffn_up"]).reshape(16, 2, 128, 1024))
    dnw = np.ascontiguousarray(np.asarray(inputs["L0_ffn_down"]).reshape(16, 8, 128, 256))

    com = {
        "qkvg_w": qkvg.astype(np.float32),
        "outw_w": outw.astype(np.float32),
        "inpj_w": inpw.astype(np.float32),
        "dtpj_w": dtw.astype(np.float32),
        "outp_w": outpw.astype(np.float32),
        "up_w": upw.astype(np.float32),
        "dn_w": dnw.astype(np.float32),
        "normw_a": np.ascontiguousarray(
            np.asarray(inputs["L0_attn_a_norm_w"]).reshape(32, 128).T).astype(np.float32),
        "normw_b": np.ascontiguousarray(
            np.asarray(inputs["L0_attn_b_norm_weight"]).reshape(32, 128).T).astype(np.float32),
        "convw": np.ascontiguousarray(
            np.asarray(inputs["L0_attn_b_conv_weight"]).reshape(48, 128, 4)).astype(np.float32),
        "convb": np.ascontiguousarray(
            np.asarray(inputs["L0_attn_b_conv_bias"]).reshape(48, 128).T).astype(np.float32),
        "dtbias": np.asarray(inputs["L0_attn_b_dt_bias"]).reshape(64, 1).astype(np.float32),
        "alog": np.asarray(inputs["L0_attn_b_A_log"]).reshape(64, 1).astype(np.float32),
        "d_bc": np.ascontiguousarray(np.broadcast_to(
            np.asarray(inputs["L0_attn_b_D"])[None, :], (128, 64))).astype(np.float32),
        "ddT": np.ascontiguousarray(dd.transpose(0, 2, 1)).astype(np.float32),
        "qd_row": np.ascontiguousarray(np.tile(qd, (1, NB))[:, None, :]).astype(_bf),
        "kd_col": np.ascontiguousarray(
            kd.reshape(HA, 2, 128).transpose(2, 1, 0).reshape(128, 2 * HA)).astype(np.float32),
        "bdpow": np.ascontiguousarray(np.broadcast_to(
            (bdec[:, None] ** np.arange(NB)[None, :]).reshape(1, HA * NB),
            (128, HA * NB))).astype(np.float32),
        "ones_col": np.ones((128, 1), _bf),
        "ones_row": np.ones((1, 128), _bf),
        "ones_row_f32": np.ones((1, 128), np.float32),
        "ones128_f32": np.ones((128, 128), np.float32),
        "ident_bf": np.eye(128, dtype=_bf),
        "ident_f32": np.eye(128, dtype=np.float32),
        "eps_col": np.full((128, 1), EPS, np.float32),
        "one_col": np.ones((128, 1), np.float32),
        "tri_neg": np.where(np.arange(128)[None, :] >= np.arange(128)[:, None],
                            0.0, -1e30).astype(np.float32),
        "tri_negT": np.where(np.arange(128)[None, :] >= np.arange(128)[:, None],
                             0.0, -1e30).astype(np.float32).T.copy(),
    }

    in_maps = []
    for c in range(NC):
        m = dict(com)
        m["x0"] = np.ascontiguousarray(xp[c * TPC:(c + 1) * TPC].T).astype(np.float32)
        wf = np.zeros((NC, HA), np.float64)
        for j in range(c):
            wf[j] = bdec ** (8 * (c - 1 - j))
        m["wfold_a"] = np.ascontiguousarray(np.broadcast_to(
            wf.reshape(1, NC * HA), (128, NC * HA))).astype(np.float32)
        mm = np.zeros((NC, NC), np.float32)
        for j in range(c):
            mm[j + 1:c, j] = 1.0
        m["mmask"] = mm.astype(np.float32)
        jm = np.zeros((NC, 1), np.float32)
        jm[:c] = 1.0
        m["jmask8"] = jm
        hm = np.zeros((128, NC), np.float32)
        if c > 0:
            hm[:, c - 1] = 1.0
        m["halo_mask"] = hm
        in_maps.append(m)
    return in_maps, perm


class Ctr:
    __slots__ = ("h", "n")

    def __init__(self, h):
        self.h = h
        self.n = 0


class KB:
    ENG = ("T", "V", "S", "G", "Y")

    def __init__(self, nc, block):
        self.nc = nc
        self.block = block
        self.ops = {e: [] for e in self.ENG}
        self.ctx = contextlib.ExitStack()
        self.sems = {}
        self.bar_n = 0
        self.own = {}
        self.last_selfwait = {}

    def sem(self, name):
        if name not in self.sems:
            self.sems[name] = Ctr(self.ctx.enter_context(self.nc.semaphore(name)))
        return self.sems[name]

    def op(self, e, fn, inc=None, inck=1):
        if e in ("V", "S"):
            own = self.own[e]
            if inc is None:
                inc = own
            if own.n > 0:
                self.ops[e].append(lambda eng, h=own.h, v=own.n: eng.wait_ge(h, v))
        if inc is not None:
            inc.n += inck

            def wrapped(eng, fn=fn, sem=inc.h, k=inck):
                fn(eng).then_inc(sem, k)
            self.ops[e].append(wrapped)
            return inc.n
        self.ops[e].append(fn)
        return None

    def wait(self, e, ctr, val):
        if val > 0:
            self.ops[e].append(lambda eng, h=ctr.h, v=val: eng.wait_ge(h, v))

    def wait_all(self, e, ctr):
        self.wait(e, ctr, ctr.n)

    def fence(self, e):
        for nm in ("pe", "ve", "ac"):
            if nm in self.sems:
                self.wait_all(e, self.sems[nm])

    def dma(self, e, out_ap, in_ap, ctr):
        return self.op(e, lambda eng, o=out_ap, i=in_ap: eng.dma_start(out=o, in_=i),
                       inc=ctr, inck=16)

    def barrier(self):
        self.bar_n += 1
        n = self.bar_n
        drains = [(nm, c, c.n) for nm, c in self.sems.items()
                  if not nm.startswith("bar_")]
        for e in self.ENG:
            for nm, c, v in drains:
                self.wait(e, c, v)
        for e in self.ENG:
            b = self.sem("bar_" + e)
            self.op(e, lambda eng: eng.nop(), inc=b)
        for e in self.ENG:
            for o in self.ENG:
                if o != e:
                    self.wait(e, self.sem("bar_" + o), n)

    def flush(self):
        ops = self.ops
        self.ops = {e: [] for e in self.ENG}
        if ops["T"]:
            self.block.tensor(lambda t, fns=ops["T"]: [f(t) for f in fns] and None)
        if ops["V"]:
            self.block.vector(lambda v, fns=ops["V"]: [f(v) for f in fns] and None)
        if ops["S"]:
            self.block.scalar(lambda s, fns=ops["S"]: [f(s) for f in fns] and None)
        if ops["G"]:
            self.block.gpsimd(lambda g, fns=ops["G"]: [f(g) for f in fns] and None)
        if ops["Y"]:
            self.block.sync(lambda y, fns=ops["Y"]: [f(y) for f in fns] and None)


DBG_SHAPES = {
    "qkv_sp": ([96, 128, TPC], BF16),
    "gate_a_sp": ([32, 128, TPC], BF16),
    "attn0_sp": ([HA, 16, 128, 128], BF16),
    "qt_sp": ([HA, 128, TPC], BF16),
    "cc_a_src": ([HA * 128, 128], BF16),
    "x1_d": ([D, TPC], BF16),
    "x2_d": ([D, TPC], BF16),
    "x3_d": ([D, TPC], BF16),
    "xbc_sp": ([48, 128, TPC], BF16),
    "gate_b_sp": ([32, 128, TPC], BF16),
    "y_sp": ([16, 128, 4096], BF16),
    "cc_b_src": ([1025, 512], F32),
    "dbg_cum": ([2, 64, TPC], F32),
    "dbg_xt": ([48, 128, 512], BF16),
    "dbg_m": ([5, 128, 1024], BF16),
    "dbg_c": ([128, 256], F32),
}


def build_program(upto="F", debug=()):
    nc = bass.Bass("TRN2", target_bir_lowering=False)

    nc_input_handles = {}

    def din(name, shape, dt=F32):
        h = nc.dram_tensor(name, shape, dt, kind="ExternalInput")
        nc_input_handles[name] = h
        return h.ap()

    x0 = din("x0", [D, TPC])
    qkvg_w = din("qkvg_w", [64, 4, 128, 8, 256])
    outw_w = din("outw_w", [16, 4, 128, 8, 256])
    inpj_w = din("inpj_w", [40, 4, 128, 8, 256])
    dtpj_w = din("dtpj_w", [4, 128, 8, 64])
    outp_w = din("outp_w", [16, 4, 128, 8, 256])
    up_w = din("up_w", [16, 2, 128, 1024])
    dn_w = din("dn_w", [16, 8, 128, 256])
    normw_a = din("normw_a", [128, 32])
    normw_b = din("normw_b", [128, 32])
    convw = din("convw", [48, 128, 4])
    convb = din("convb", [128, 48])
    dtbias = din("dtbias", [64, 1])
    alog = din("alog", [64, 1])
    d_bc = din("d_bc", [128, 64])
    ddT = din("ddT", [HA, 256, 256])
    qd_row = din("qd_row", [HA, 1, TPC], BF16)
    kd_col = din("kd_col", [128, 2 * HA])
    bdpow = din("bdpow", [128, HA * NB])
    ones_col = din("ones_col", [128, 1], BF16)
    ones_row = din("ones_row", [1, 128], BF16)
    ones_row_f32 = din("ones_row_f32", [1, 128])
    ones128_f32 = din("ones128_f32", [128, 128])
    ident_bf = din("ident_bf", [128, 128], BF16)
    ident_f32 = din("ident_f32", [128, 128])
    eps_col = din("eps_col", [128, 1])
    one_col = din("one_col", [128, 1])
    tri_neg = din("tri_neg", [128, 128])
    tri_negT = din("tri_negT", [128, 128])
    wfold_a = din("wfold_a", [128, NC * HA])
    mmask = din("mmask", [NC, NC])
    jmask8 = din("jmask8", [NC, 1])
    halo_mask = din("halo_mask", [128, NC])

    out_fm = nc.dram_tensor("out_fm", [D, TPC], F32, kind="ExternalOutput").ap()

    idram = {}
    idram_h = {}
    for name, (shape, dt) in DBG_SHAPES.items():
        kind = "ExternalOutput" if name in debug else "Internal"
        h = nc.dram_tensor(name, shape, dt, kind=kind)
        idram_h[name] = h
        idram[name] = h.ap()
    qkv_sp = idram["qkv_sp"]
    gate_a_sp = idram["gate_a_sp"]
    attn0_sp = idram["attn0_sp"]
    attn0_sp_h = idram_h["attn0_sp"]
    qt_sp = idram["qt_sp"]
    qt_sp_h = idram_h["qt_sp"]
    cc_a_src = idram["cc_a_src"]
    x1_d = idram["x1_d"]
    x2_d = idram["x2_d"]
    x3_d = idram["x3_d"]
    xbc_sp = idram["xbc_sp"]
    gate_b_sp = idram["gate_b_sp"]
    y_sp = idram["y_sp"]
    cc_b_src = idram["cc_b_src"]
    cc_b_src_h = idram_h["cc_b_src"]
    cc_a_dst_h = nc.dram_tensor("cc_a_dst", [NC * HA * 128, 128], BF16, kind="Internal")
    cc_a_dst = cc_a_dst_h
    cc_h_src = nc.dram_tensor("cc_h_src", [128, 192], F32, kind="Internal")
    cc_h_dst_h = nc.dram_tensor("cc_h_dst", [NC * 128, 192], F32, kind="Internal")
    cc_h_dst = cc_h_dst_h
    cc_b_dst_h = nc.dram_tensor("cc_b_dst", [NC * 1025, 512], F32, kind="Internal")
    cc_b_dst = cc_b_dst_h
    inpj_w_h = nc_input_handles["inpj_w"]
    dn_w_h = nc_input_handles["dn_w"]

    with nc.Block() as block:
        kb = KB(nc, block)
        ctx = kb.ctx
        qw = kb.sem("qw")
        qx = kb.sem("qx")
        qs = kb.sem("qs")
        qg = kb.sem("qg")
        pe = kb.sem("pe")
        ve = kb.sem("ve")
        ac = kb.sem("ac")
        cc = kb.sem("cc")
        kb.own = {"V": ve, "S": ac}

        STREAM = ctx.enter_context(nc.sbuf_tensor("STREAM", [128, NKC * TPC], BF16))
        KC = {}
        for nm, ap_, shape, dt in (
            ("ones_col", ones_col, [128, 1], BF16),
            ("ones_row", ones_row, [1, 128], BF16),
            ("ones_row_f", ones_row_f32, [1, 128], F32),
            ("ones128_f", ones128_f32, [128, 128], F32),
            ("ident_bf", ident_bf, [128, 128], BF16),
            ("ident_f", ident_f32, [128, 128], F32),
            ("eps_col", eps_col, [128, 1], F32),
            ("one_col", one_col, [128, 1], F32),
            ("tri_neg", tri_neg, [128, 128], F32),
            ("tri_negT", tri_negT, [128, 128], F32),
            ("norm_a", normw_a, [128, 32], F32),
            ("norm_b", normw_b, [128, 32], F32),
            ("d_bc", d_bc, [128, 64], F32),
            ("bdpow", bdpow, [128, HA * NB], F32),
            ("wfold_a", wfold_a, [128, NC * HA], F32),
            ("convb", convb, [128, 48], F32),
            ("convw", bass.AP(nc_input_handles["convw"], 0,
                              [[4, 128], [512, 48], [1, 4]]), [128, 192], F32),
            ("dtbias", dtbias, [64, 1], F32),
            ("alog", alog, [64, 1], F32),
            ("jmask8", jmask8, [NC, 1], F32),
            ("halo_mask", halo_mask, [128, NC], F32),
            ("mmask", mmask, [NC, NC], F32),
            ("kd_col", kd_col, [128, 2 * HA], F32),
        ):
            t = ctx.enter_context(nc.sbuf_tensor("K_" + nm, shape, dt))
            kb.dma("G", t[tuple(slice(None) for _ in shape)], ap_, qw)
            KC[nm] = t
        consts_tok = qw.n
        K_stats = ctx.enter_context(nc.sbuf_tensor("K_stats", [1, 2 * TPC], F32))

        _lnid = [0]

        def emit_ln_stats(slot, ready_ctr, ready_tok):
            rs = 0
            _lnid[0] += 1
            slot = _lnid[0]
            with contextlib.ExitStack() as _es:
                sq = _es.enter_context(nc.sbuf_tensor(f"sqr{slot}", [128, 2 * TPC], BF16))
                srow = _es.enter_context(nc.sbuf_tensor(f"srow{slot}", [1, 2 * TPC], F32))
                _ = 0
                kb.wait_all("T", ready_ctr)
                kb.wait_all("S", ready_ctr)
                with contextlib.ExitStack() as ps_ctx:
                    sps = [ps_ctx.enter_context(
                        nc.psum_tensor(f"sps{slot}_{i}", [1, 512], F32)) for i in range(4)]
                    for kc in range(NKC):
                        for tsub in range(4):
                            tok = kb.op("T", lambda t, kc=kc, tsub=tsub: t.matmul(
                                sps[tsub][:], KC["ones_col"][:],
                                STREAM[:, kc * TPC + tsub * 512: kc * TPC + (tsub + 1) * 512],
                                start=(kc == 0), stop=(kc == NKC - 1)),
                                inc=pe if (kc == NKC - 1) else None)
                    kb.wait("V", pe, pe.n)
                    for tsub in range(4):
                        kb.op("V", lambda v, tsub=tsub: v.tensor_copy(
                            srow[:, tsub * 512:(tsub + 1) * 512], sps[tsub][:]), inc=ve)
                    sums_done = ve.n
                    kb.flush()
                with contextlib.ExitStack() as ps_ctx:
                    sps = [ps_ctx.enter_context(
                        nc.psum_tensor(f"qps{slot}_{i}", [1, 512], F32)) for i in range(4)]
                    kb.wait("T", ve, sums_done)
                    ring_tok = [0, 0]
                    for kc in range(NKC):
                        slot2 = kc % 2
                        kb.wait("S", pe, ring_tok[slot2])
                        sq_tok = kb.op("S", lambda s, kc=kc, sl=slot2: s.activation(
                            sq[:, sl * TPC:(sl + 1) * TPC],
                            STREAM[:, kc * TPC:(kc + 1) * TPC], AF.Square), inc=ac)
                        kb.wait("T", ac, sq_tok)
                        for tsub in range(4):
                            kb.op("T", lambda t, sl=slot2, tsub=tsub, kc=kc: t.matmul(
                                sps[tsub][:], KC["ones_col"][:],
                                sq[:, sl * TPC + tsub * 512: sl * TPC + (tsub + 1) * 512],
                                start=(kc == 0), stop=(kc == NKC - 1)),
                                inc=pe)
                        ring_tok[slot2] = pe.n
                    kb.wait("V", pe, pe.n)
                    for tsub in range(4):
                        kb.op("V", lambda v, tsub=tsub: v.tensor_copy(
                            srow[:, TPC + tsub * 512:TPC + (tsub + 1) * 512],
                            sps[tsub][:]), inc=ve)
                    kb.flush()
                kb.wait("V", ve, ve.n)
                kb.op("V", lambda v: v.tensor_scalar(
                    srow[:, 0:TPC], srow[:, 0:TPC], 1.0 / D, 0.0, OP.mult, OP.add), inc=ve)
                kb.op("V", lambda v: v.tensor_scalar(
                    srow[:, TPC:], srow[:, TPC:], 1.0 / D, 0.0, OP.mult, OP.add), inc=ve)
                kb.op("V", lambda v: v.tensor_tensor(
                    K_stats[:, rs:rs + TPC], srow[:, 0:TPC], srow[:, 0:TPC], OP.mult), inc=ve)
                kb.op("V", lambda v: v.tensor_tensor(
                    K_stats[:, rs:rs + TPC], srow[:, TPC:], K_stats[:, rs:rs + TPC],
                    OP.subtract), inc=ve)
                tok_v = ve.n
                kb.wait("S", ve, tok_v)
                kb.op("S", lambda s: s.activation(
                    K_stats[:, rs:rs + TPC], K_stats[:, rs:rs + TPC], AF.Ln,
                    bias=KC["eps_col"][0:1, :]), inc=ac)
                kb.op("S", lambda s: s.activation(
                    K_stats[:, rs:rs + TPC], K_stats[:, rs:rs + TPC], AF.Exp,
                    scale=-0.5), inc=ac)
                tok_r = ac.n
                kb.wait("V", ac, tok_r)
                kb.op("V", lambda v: v.tensor_tensor(
                    K_stats[:, rs + TPC:rs + 2 * TPC], srow[:, 0:TPC],
                    K_stats[:, rs:rs + TPC], OP.mult), inc=ve)
                kb.op("V", lambda v: v.tensor_scalar(
                    K_stats[:, rs + TPC:rs + 2 * TPC], K_stats[:, rs + TPC:rs + 2 * TPC],
                    -1.0, 0.0, OP.mult, OP.add), inc=ve)
                kb.flush()
            return ve.n

        def emit_ln_apply(slot, stats_tok):
            rs = 0
            slot = _lnid[0]
            with contextlib.ExitStack() as _es:
                bc = _es.enter_context(nc.sbuf_tensor(f"bc{slot}", [128, 2 * TPC], BF16))
                tmp = _es.enter_context(nc.sbuf_tensor(f"tmpa{slot}", [128, TPC], BF16))
                bcps = _es.enter_context(nc.psum_tensor(f"bcps{slot}", [128, 512], F32))
                _ = 0
                kb.wait("T", ve, stats_tok)
                for r in range(2):
                    for tsub in range(4):
                        mt = kb.op("T", lambda t, r=r, tsub=tsub: t.matmul(
                            bcps[:], KC["ones_row_f"][:],
                            K_stats[:, rs + r * TPC + tsub * 512: rs + r * TPC + (tsub + 1) * 512],
                            start=True, stop=True), inc=pe)
                        kb.wait("V", pe, mt)
                        et = kb.op("V", lambda v, r=r, tsub=tsub: v.tensor_copy(
                            bc[:, r * TPC + tsub * 512: r * TPC + (tsub + 1) * 512],
                            bcps[:]), inc=ve)
                        kb.wait("T", ve, et)
                for kc in range(NKC):
                    kb.op("V", lambda v, kc=kc: v.tensor_tensor(
                        tmp[:], STREAM[:, kc * TPC:(kc + 1) * TPC], bc[:, 0:TPC],
                        OP.mult), inc=ve)
                    kb.op("V", lambda v, kc=kc: v.tensor_tensor(
                        STREAM[:, kc * TPC:(kc + 1) * TPC], tmp[:], bc[:, TPC:],
                        OP.add), inc=ve)
                kb.flush()
            return ve.n

        def emit_proj(w_ap, npair, rhs_ready_tok, evac_fn, name):
            with (
                nc.sbuf_tensor(f"wr_{name}", [128, 2 * 8192], BF16) as wring,
                contextlib.ExitStack() as ps_ctx,
            ):
                mps = [ps_ctx.enter_context(
                    nc.psum_tensor(f"mps_{name}_{i}", [128, 512], F32)) for i in range(4)]
                wsem = [kb.sem(f"qw0_{name}"), kb.sem(f"qw1_{name}")]
                slot_tok = [0, 0]
                bank_free = [0, 0, 0, 0]
                bank_owner = [None] * 4
                kb.wait("T", ve, rhs_ready_tok)
                kb.wait("G", ve, rhs_ready_tok)
                for fp in range(npair):
                    sl = fp % 2
                    kb.wait("G", pe, slot_tok[sl])
                    for kg in range(4):
                        kb.dma("G", wring[:, sl * 8192 + kg * 2048: sl * 8192 + (kg + 1) * 2048],
                               w_ap[fp, kg].rearrange("c k j -> c (k j)"), wsem[sl])
                    kb.wait_all("T", wsem[sl])
                    for f2 in range(2):
                        fc = fp * 2 + f2
                        for tsub in range(4):
                            bk = (fc * 4 + tsub) % 4
                            if bank_owner[bk] is not None:
                                kb.wait("T", bank_owner[bk], bank_free[bk])
                            for kc in range(NKC):
                                kb.op("T", lambda t, sl=sl, kc=kc, f2=f2, tsub=tsub, bk=bk: t.matmul(
                                    mps[bk][:],
                                    wring[:, sl * 8192 + (kc % 8) * 256 + (kc // 8) * 2048
                                          + f2 * 128: sl * 8192 + (kc % 8) * 256
                                          + (kc // 8) * 2048 + f2 * 128 + 128],
                                    STREAM[:, kc * TPC + tsub * 512: kc * TPC + (tsub + 1) * 512],
                                    start=(kc == 0), stop=(kc == NKC - 1)),
                                    inc=pe if kc == NKC - 1 else None)
                            mm_tok = pe.n
                            ectr, etok = evac_fn(fc, tsub, mps[bk], mm_tok)
                            bank_owner[bk] = ectr
                            bank_free[bk] = etok
                    slot_tok[sl] = pe.n
                kb.flush()

        for kc in range(NKC):
            kb.dma("G", STREAM[:, kc * TPC:(kc + 1) * TPC], x0[kc * 128:(kc + 1) * 128, :], qw)
        x0_tok = qw.n
        st_tok = emit_ln_stats(0, qw, x0_tok)
        ln1_tok = emit_ln_apply(0, st_tok)

        with nc.sbuf_tensor("ev_a1", [128, 4 * 512], BF16) as etile:
            ering = [(None, 0)] * 4

            def evac_a1(fc, tsub, ps, mm_tok):
                ei = (fc * 4 + tsub) % 4
                own, tok = ering[ei]
                if own is not None:
                    kb.wait_all("S", own)
                kb.wait("S", pe, mm_tok)
                func = AF.Silu if fc < 96 else AF.Sigmoid
                et = kb.op("S", lambda s, ei=ei, ps=ps, func=func: s.activation(
                    etile[:, ei * 512:(ei + 1) * 512], ps[:], func), inc=ac)
                kb.wait("Y", ac, et)
                dst = (qkv_sp[fc, :, tsub * 512:(tsub + 1) * 512] if fc < 96
                       else gate_a_sp[fc - 96, :, tsub * 512:(tsub + 1) * 512])
                stok = kb.dma("Y", dst, etile[:, ei * 512:(ei + 1) * 512], qs)
                ering[ei] = (qs, stok)
                return ac, et

            emit_proj(qkvg_w, 64, ln1_tok, evac_a1, "a1")
        kb.barrier()

        PH = ["A1", "A2", "B", "C", "D1", "D2", "E", "F"]
        LAST = PH.index(upto)

        def APX(handle, off, pat):
            return bass.AP(handle, off, pat)

        PT = {}

        def phase_a2():
            PT["sinit_es"] = contextlib.ExitStack()
            PT["sinit_a"] = PT["sinit_es"].enter_context(
                nc.sbuf_tensor("sinit_a", [128, HA * 128], BF16))
            sinit_a = PT["sinit_a"]
            qh = kb.sem("qh")
            bd = _attn_consts()[3]
            qtt = STREAM[:, 6144:8192]
            qdb = STREAM[:, 8192:10240]
            scm = STREAM[:, 10240:10752]
            vtm = STREAM[:, 10752:11008]
            ktm = STREAM[:, 11008:11264]
            kvt = STREAM[:, 11264:11520]
            a0t = STREAM[:, 11520:11776]
            gath = STREAM[:, 12288:16384]
            qkv_base = [0, 16384]
            with contextlib.ExitStack() as _es:
                ddt = _es.enter_context(nc.sbuf_tensor("a2_dd", [128, 1024], F32))
                qdr = _es.enter_context(nc.sbuf_tensor("a2_qdr", [1, 2 * TPC], BF16))
                scps = _es.enter_context(nc.psum_tensor("a2_scps", [128, 512], F32))
                tps = _es.enter_context(nc.psum_tensor("a2_tps", [128, 512], BF16))
                ops_ = _es.enter_context(nc.psum_tensor("a2_ops", [128, 256], F32))
                ups = _es.enter_context(nc.psum_tensor("a2_ups", [128, 128], F32))
                bcps = _es.enter_context(nc.psum_tensor("a2_bcps", [128, 2048], F32))
                cons_t = [0, 0]
                cons_v = [0, 0]
                qh2 = [kb.sem("qha"), kb.sem("qhb")]

                def a2_load(hh):
                    sl2 = hh % 2
                    kb.wait("Y", pe, cons_t[sl2])
                    kb.wait("Y", ve, cons_v[sl2])
                    for i in range(3):
                        kb.dma("Y", STREAM[:, qkv_base[sl2] + i * TPC: qkv_base[sl2] + (i + 1) * TPC],
                               qkv_sp[3 * hh + i], qh2[sl2])
                    for mc in range(2):
                        kb.dma("Y", ddt[:, sl2 * 512 + mc * 256: sl2 * 512 + (mc + 1) * 256],
                               ddT[hh, mc * 128:(mc + 1) * 128, :], qh2[sl2])
                    kb.dma("Y", qdr[:, sl2 * TPC:(sl2 + 1) * TPC], qd_row[hh], qh2[sl2])

                a2_load(0)
                for h in range(HA):
                    sl = h % 2
                    qkvt = STREAM[:, qkv_base[sl]: qkv_base[sl] + 6144]
                    ddtv = ddt[:, sl * 512:(sl + 1) * 512]
                    qdrv = qdr[:, sl * TPC:(sl + 1) * TPC]
                    if h + 1 < HA:
                        a2_load(h + 1)
                    kb.wait_all("Y", qs)
                    kb.wait_all("V", qs)
                    kb.wait_all("T", qh2[sl])
                    kb.wait_all("V", qh2[sl])
                    for i in range(4):
                        kb.op("T", lambda t, i=i, qdrv=qdrv: t.matmul(
                            bcps[:, i * 512:(i + 1) * 512], KC["ones_row"][:],
                            qdrv[:, i * 512:(i + 1) * 512],
                            start=True, stop=True), inc=pe)
                    kb.wait("V", pe, pe.n)
                    for i in range(4):
                        kb.op("V", lambda v, i=i: v.tensor_copy(
                            qdb[:, i * 512:(i + 1) * 512], bcps[:, i * 512:(i + 1) * 512]), inc=ve)
                    kb.op("V", lambda v, qkvt=qkvt: v.tensor_tensor(
                        qtt[:, :], qkvt[:, 0:TPC], qdb[:, :], OP.mult), inc=ve)
                    qt_tok = ve.n
                    kb.wait("Y", ve, qt_tok)
                    kb.dma("Y", qt_sp[h], qtt[:, :], qs)
                    kb.op("V", lambda v: v.memset(kvt[:, 0:128], 0.0), inc=ve)
                    kv_tok = ve.n
                    for b in range(NB):
                        nb0 = b * BS
                        cur, nxt = (b % 2) * 128, ((b + 1) % 2) * 128
                        for mc in range(2):
                            kb.op("T", lambda t, mc=mc, nb0=nb0, qkvt=qkvt: t.matmul(
                                scps[:, mc * 256:(mc + 1) * 256],
                                qkvt[:, TPC + nb0 + mc * 128: TPC + nb0 + (mc + 1) * 128],
                                qkvt[:, nb0: nb0 + 256],
                                start=True, stop=True), inc=pe)
                        for mc in range(2):
                            kb.op("T", lambda t, mc=mc, nb0=nb0, qkvt=qkvt: t.transpose(
                                tps[:, mc * 128:(mc + 1) * 128],
                                qkvt[:, 2 * TPC + nb0 + mc * 128: 2 * TPC + nb0 + (mc + 1) * 128],
                                KC["ident_bf"][:]), inc=pe)
                            kb.op("T", lambda t, mc=mc, nb0=nb0, qkvt=qkvt: t.transpose(
                                tps[:, 256 + mc * 128: 256 + (mc + 1) * 128],
                                qkvt[:, TPC + nb0 + mc * 128: TPC + nb0 + (mc + 1) * 128],
                                KC["ident_bf"][:]), inc=pe)
                        kb.wait("V", pe, pe.n)
                        for mc in range(2):
                            kb.op("V", lambda v, mc=mc, ddtv=ddtv: v.tensor_tensor(
                                scm[:, mc * 256:(mc + 1) * 256],
                                scps[:, mc * 256:(mc + 1) * 256],
                                ddtv[:, mc * 256:(mc + 1) * 256], OP.mult), inc=ve)
                            kb.op("V", lambda v, mc=mc: v.tensor_copy(
                                vtm[:, mc * 128:(mc + 1) * 128],
                                tps[:, mc * 128:(mc + 1) * 128]), inc=ve)
                            kb.op("V", lambda v, mc=mc, h=h: v.tensor_scalar(
                                ktm[:, mc * 128:(mc + 1) * 128],
                                tps[:, 256 + mc * 128: 256 + (mc + 1) * 128],
                                KC["kd_col"][:, mc * 32 + h: mc * 32 + h + 1], 0.0,
                                OP.mult, OP.add), inc=ve)
                        prep_tok = ve.n
                        kb.wait("T", ve, prep_tok)
                        for nc2 in range(2):
                            kb.op("T", lambda t, nc2=nc2: t.matmul(
                                ops_[:, nc2 * 128:(nc2 + 1) * 128],
                                scm[:, nc2 * 128: nc2 * 128 + 128],
                                vtm[:, 0:128], start=True, stop=False))
                            kb.op("T", lambda t, nc2=nc2: t.matmul(
                                ops_[:, nc2 * 128:(nc2 + 1) * 128],
                                scm[:, 256 + nc2 * 128: 256 + nc2 * 128 + 128],
                                vtm[:, 128:256], start=False, stop=False))
                            kb.op("T", lambda t, nc2=nc2, nb0=nb0, cur=cur, qkvt=qkvt: t.matmul(
                                ops_[:, nc2 * 128:(nc2 + 1) * 128],
                                qtt[:, nb0 + nc2 * 128: nb0 + (nc2 + 1) * 128],
                                kvt[:, cur:cur + 128], start=False, stop=True),
                                inc=pe if nc2 == 1 else None)
                        for mc in range(2):
                            kb.op("T", lambda t, mc=mc: t.matmul(
                                ups[:], ktm[:, mc * 128:(mc + 1) * 128],
                                vtm[:, mc * 128:(mc + 1) * 128],
                                start=(mc == 0), stop=(mc == 1)),
                                inc=pe if mc == 1 else None)
                        kb.wait("V", pe, pe.n)
                        kb.wait_all("V", qs)
                        for nc2 in range(2):
                            et = kb.op("V", lambda v, nc2=nc2: v.tensor_copy(
                                a0t[:, nc2 * 128:(nc2 + 1) * 128],
                                ops_[:, nc2 * 128:(nc2 + 1) * 128]), inc=ve)
                        kb.op("V", lambda v, cur=cur, nxt=nxt, bdh=float(bd[h]): v.scalar_tensor_tensor(
                            kvt[:, nxt:nxt + 128], kvt[:, cur:cur + 128], bdh, ups[:],
                            OP.mult, OP.add), inc=ve)
                        kv_tok = ve.n
                        kb.wait("Y", ve, kv_tok)
                        for nc2 in range(2):
                            kb.dma("Y", attn0_sp[h, 2 * b + nc2],
                                   a0t[:, nc2 * 128:(nc2 + 1) * 128], qs)
                        kb.wait("T", ve, kv_tok)
                    kb.wait("Y", ve, kv_tok)
                    kb.dma("Y", cc_a_src[(h * 128):(h + 1) * 128, :],
                           kvt[:, (NB % 2) * 128:(NB % 2) * 128 + 128], qs)
                    cons_t[sl] = pe.n
                    cons_v[sl] = ve.n
                kb.flush()
                kb.wait_all("G", qs)
                gtok = kb.op("G", lambda g: g.collective_compute(
                    "AllGather", OP.bypass, replica_groups=[list(range(NC))],
                    ins=[cc_a_src.opt()], outs=[cc_a_dst.ap().opt()]), inc=cc)
                kb.wait("G", cc, gtok)
                kb.op("V", lambda v: v.memset(sinit_a[:, :], 0.0), inc=ve)
                for j in range(NC):
                    kb.wait("G", ve, ve.n)
                    kb.dma("G", gath,
                           APX(cc_a_dst_h, j * HA * 128 * 128,
                               [[128, 128], [128 * 128, HA], [1, 128]]), qg)
                    kb.wait_all("V", qg)
                    for h in range(HA):
                        kb.op("V", lambda v, j=j, h=h: v.scalar_tensor_tensor(
                            sinit_a[:, h * 128:(h + 1) * 128],
                            gath[:, h * 128:(h + 1) * 128],
                            KC["wfold_a"][:, j * HA + h: j * HA + h + 1],
                            sinit_a[:, h * 128:(h + 1) * 128],
                            OP.mult, OP.add), inc=ve)
                kb.flush()

        def phase_b():
            sinit_a = PT["sinit_a"]
            qb = kb.sem("qb")
            with contextlib.ExitStack() as _es:
                a0 = _es.enter_context(nc.sbuf_tensor("b_a0", [128, 4096], BF16))
                qtl = _es.enter_context(nc.sbuf_tensor("b_qt", [128, 4096], BF16))
                sb = _es.enter_context(nc.sbuf_tensor("b_sb", [128, 4096], BF16))
                af = _es.enter_context(nc.sbuf_tensor("b_af", [128, 4096], BF16))
                sq = _es.enter_context(nc.sbuf_tensor("b_sq", [128, 4096], BF16))
                rr = _es.enter_context(nc.sbuf_tensor("b_r", [128, 1], F32))
                crps = _es.enter_context(nc.psum_tensor("b_crps", [128, 2048], F32))
                btp_a = _es.enter_context(nc.psum_tensor("b_tp_a", [128, 512], BF16))
                btp_b = _es.enter_context(nc.psum_tensor("b_tp_b", [128, 512], BF16))
                btp2 = [btp_a, btp_b]
                cons_t = 0
                cons_v = 0
                for tc in range(16):
                    b = tc // 2
                    if tc % 2 == 0:
                        kb.wait("V", pe, cons_t)
                        for h in range(HA):
                            kb.op("V", lambda v, h=h, b=b: v.tensor_scalar(
                                sb[:, h * 128:(h + 1) * 128],
                                sinit_a[:, h * 128:(h + 1) * 128],
                                KC["bdpow"][:, h * NB + b: h * NB + b + 1], 0.0,
                                OP.mult, OP.add), inc=ve)
                    sb_tok = ve.n
                    kb.wait("Y", pe, cons_t)
                    kb.wait("Y", ve, cons_v)
                    for hg in range(4):
                        kb.dma("Y", a0[:, hg * 1024:(hg + 1) * 1024],
                               APX(attn0_sp_h, hg * 8 * 16 * 128 * 128 + tc * 128 * 128,
                                   [[128, 128], [16 * 128 * 128, 8], [1, 128]]), qb)
                        kb.dma("Y", qtl[:, hg * 1024:(hg + 1) * 1024],
                               APX(qt_sp_h, hg * 8 * 128 * TPC + tc * 128,
                                   [[TPC, 128], [128 * TPC, 8], [1, 128]]), qb)
                    kb.wait_all("T", qb)
                    kb.wait("T", ve, sb_tok)
                    kb.wait_all("V", qb)
                    for hg in range(4):
                        if hg >= 2:
                            kb.wait("T", ve, ve.n)
                        for h8 in range(8):
                            h = hg * 8 + h8
                            kb.op("T", lambda t, h=h, hg=hg, h8=h8: t.matmul(
                                crps[:, (hg % 2) * 1024 + h8 * 128: (hg % 2) * 1024 + (h8 + 1) * 128],
                                qtl[:, h * 128:(h + 1) * 128],
                                sb[:, h * 128:(h + 1) * 128],
                                start=True, stop=True), inc=pe)
                        kb.wait("V", pe, pe.n)
                        for h8 in range(8):
                            h = hg * 8 + h8
                            kb.op("V", lambda v, h=h, hg=hg, h8=h8: v.tensor_tensor(
                                af[:, h * 128:(h + 1) * 128],
                                crps[:, (hg % 2) * 1024 + h8 * 128: (hg % 2) * 1024 + (h8 + 1) * 128],
                                a0[:, h * 128:(h + 1) * 128], OP.add), inc=ve)
                    cons_t = pe.n
                    kb.op("V", lambda v: v.tensor_tensor(sq[:, :], af[:, :], af[:, :], OP.mult), inc=ve)
                    kb.op("V", lambda v: v.reduce_sum(rr[:], sq[:, :], axis=AX.X), inc=ve)
                    rtok = ve.n
                    kb.wait("S", ve, rtok)
                    kb.op("S", lambda s: s.activation(rr[:], rr[:], AF.Ln,
                                                      bias=KC["eps_col"][:], scale=1.0 / 4096.0), inc=ac)
                    sa = kb.op("S", lambda s: s.activation(rr[:], rr[:], AF.Exp, scale=-0.5), inc=ac)
                    kb.wait("V", ac, sa)
                    kb.op("V", lambda v: v.tensor_scalar(af[:, :], af[:, :], rr[:], 0.0,
                                                         OP.mult, OP.add), inc=ve)
                    ntok = ve.n
                    cons_v = ve.n
                    kb.wait("T", ve, ntok)
                    for fq in range(8):
                        if fq >= 2:
                            kb.wait("T", ve, ve.n)
                        for f4 in range(4):
                            fc = fq * 4 + f4
                            kb.op("T", lambda t, fc=fc, fq=fq, f4=f4: t.transpose(
                                btp2[fq % 2][:, f4 * 128:(f4 + 1) * 128],
                                af[:, fc * 128:(fc + 1) * 128], KC["ident_bf"][:]), inc=pe)
                        kb.wait("V", pe, pe.n)
                        for f4 in range(4):
                            fc = fq * 4 + f4
                            kb.op("V", lambda v, fc=fc, tc=tc, fq=fq, f4=f4: v.tensor_scalar(
                                STREAM[:, fc * TPC + tc * 128: fc * TPC + (tc + 1) * 128],
                                btp2[fq % 2][:, f4 * 128:(f4 + 1) * 128],
                                KC["norm_a"][:, fc:fc + 1], 0.0, OP.mult, OP.add), inc=ve)
                kb.flush()
            with contextlib.ExitStack() as _es:
                gt = _es.enter_context(nc.sbuf_tensor("b_g", [128, 2 * TPC], BF16))
                gsem = [kb.sem("qbg0"), kb.sem("qbg1")]
                kb.wait_all("Y", qb)
                kb.fence("Y")
                cons = [0, 0]
                for fc in range(NKC):
                    sl = fc % 2
                    kb.wait("Y", ve, cons[sl])
                    kb.dma("Y", gt[:, sl * TPC:(sl + 1) * TPC], gate_a_sp[fc], gsem[sl])
                    kb.wait_all("V", gsem[sl])
                    kb.op("V", lambda v, fc=fc, sl=sl: v.tensor_tensor(
                        STREAM[:, fc * TPC:(fc + 1) * TPC],
                        STREAM[:, fc * TPC:(fc + 1) * TPC],
                        gt[:, sl * TPC:(sl + 1) * TPC], OP.mult), inc=ve)
                    cons[sl] = ve.n
                rhs_tok = ve.n
                kb.flush()
            with contextlib.ExitStack() as _es:
                x0c = _es.enter_context(nc.sbuf_tensor("b_x0", [128, TPC], F32))
                x1t = _es.enter_context(nc.sbuf_tensor("b_x1", [128, 4 * 512], BF16))
                xsem = kb.sem("qbx")
                kb.wait_all("Y", kb.sem("qbg0"))
                kb.wait_all("Y", kb.sem("qbg1"))
                kb.fence("Y")
                st = {"xcons": 0}
                ering = [0, 0, 0, 0]

                def evac_b3(fc, tsub, ps, mm_tok):
                    if tsub == 0:
                        kb.wait("Y", ve, st["xcons"])
                        kb.dma("Y", x0c[:, :], x0[fc * 128:(fc + 1) * 128, :], xsem)
                    ei = (fc * 4 + tsub) % 4
                    kb.wait_all("V", qs)
                    kb.wait_all("V", xsem)
                    kb.wait("V", pe, mm_tok)
                    et = kb.op("V", lambda v, ps=ps, tsub=tsub, ei=ei: v.tensor_tensor(
                        x1t[:, ei * 512:(ei + 1) * 512], ps[:],
                        x0c[:, tsub * 512:(tsub + 1) * 512], OP.add), inc=ve)
                    if tsub == 3:
                        st["xcons"] = ve.n
                    kb.wait("Y", ve, et)
                    kb.dma("Y", x1_d[fc * 128:(fc + 1) * 128, tsub * 512:(tsub + 1) * 512],
                           x1t[:, ei * 512:(ei + 1) * 512], qs)
                    ering[ei] = qs.n
                    return ve, et

                emit_proj(outw_w, 16, rhs_tok, evac_b3, "b3")
                kb.flush()
            PT["sinit_es"].close()

        def phase_ffn(x_in_ap, x_out_ap, name, final=False):
            for kc in range(NKC):
                kb.dma("Y", STREAM[:, kc * TPC:(kc + 1) * TPC],
                       x_in_ap[kc * 128:(kc + 1) * 128, :], qx)
            xtok = qx.n
            st_tok = emit_ln_stats(0, qx, xtok)
            qf = [kb.sem(f"qf0_{name}"), kb.sem(f"qf1_{name}")]
            with contextlib.ExitStack() as _es:
                bc = _es.enter_context(nc.sbuf_tensor(f"f_bc_{name}", [128, 2 * TPC], BF16))
                lnx = _es.enter_context(nc.sbuf_tensor(f"f_lnx_{name}", [128, 2 * TPC], BF16))
                wt = _es.enter_context(nc.sbuf_tensor(f"f_w_{name}", [128, 2 * 4096], BF16))
                zt = _es.enter_context(nc.sbuf_tensor(f"f_z_{name}", [128, 4096], BF16))
                ot = _es.enter_context(nc.sbuf_tensor(f"f_o_{name}", [128, 4 * 512],
                                                      F32 if final else BF16))
                fps = _es.enter_context(nc.psum_tensor(f"f_ps_{name}", [128, 4 * 512], F32))
                fps2 = _es.enter_context(nc.psum_tensor(f"f_ps2_{name}", [128, 2 * 512], F32))
                bcps = _es.enter_context(nc.psum_tensor(f"f_bcps_{name}", [128, 512], F32))
                kb.wait("T", ve, st_tok)
                for r in range(2):
                    for tsub in range(4):
                        mt = kb.op("T", lambda t, r=r, tsub=tsub: t.matmul(
                            bcps[:], KC["ones_row_f"][:],
                            K_stats[:, r * TPC + tsub * 512: r * TPC + (tsub + 1) * 512],
                            start=True, stop=True), inc=pe)
                        kb.wait("V", pe, mt)
                        et = kb.op("V", lambda v, r=r, tsub=tsub: v.tensor_copy(
                            bc[:, r * TPC + tsub * 512: r * TPC + (tsub + 1) * 512],
                            bcps[:]), inc=ve)
                        kb.wait("T", ve, et)
                wcons = [0, 0]
                ocons = [0, 0, 0, 0]
                up_free = [0, 0, 0, 0]
                dn_free = [0, 0]
                kb.wait("G", ve, st_tok)
                kb.fence("G")
                for h in range(16):
                    sl = h % 2
                    kb.wait("G", pe, wcons[sl])
                    for kc2 in range(2):
                        kb.dma("G", wt[:, sl * 4096 + kc2 * 1024: sl * 4096 + (kc2 + 1) * 1024],
                               up_w[h, kc2], qf[sl])
                    kb.dma("G", wt[:, sl * 4096 + 2048: sl * 4096 + 4096],
                           APX(dn_w_h, h * 8 * 128 * 256,
                               [[256, 128], [128 * 256, 8], [1, 256]]), qf[sl])
                    for d2 in range(2):
                        fc = 2 * h + d2
                        kb.op("V", lambda v, fc=fc, d2=d2: v.tensor_tensor(
                            lnx[:, d2 * TPC:(d2 + 1) * TPC],
                            STREAM[:, fc * TPC:(fc + 1) * TPC], bc[:, 0:TPC], OP.mult), inc=ve)
                        kb.op("V", lambda v, d2=d2: v.tensor_tensor(
                            lnx[:, d2 * TPC:(d2 + 1) * TPC],
                            lnx[:, d2 * TPC:(d2 + 1) * TPC], bc[:, TPC:], OP.add), inc=ve)
                    ln_tok = ve.n
                    kb.wait("T", ve, ln_tok)
                    kb.wait_all("T", qf[sl])
                    for tsub in range(4):
                        for m in range(8):
                            pslot = m % 4
                            if up_free[pslot]:
                                kb.wait("T", ac, up_free[pslot])
                            for kc2 in range(2):
                                kb.op("T", lambda t, sl=sl, m=m, kc2=kc2, tsub=tsub, pslot=pslot: t.matmul(
                                    fps[:, pslot * 512:(pslot + 1) * 512],
                                    wt[:, sl * 4096 + kc2 * 1024 + m * 128: sl * 4096 + kc2 * 1024 + m * 128 + 128],
                                    lnx[:, kc2 * TPC + tsub * 512: kc2 * TPC + (tsub + 1) * 512],
                                    start=(kc2 == 0), stop=(kc2 == 1)),
                                    inc=pe if kc2 == 1 else None)
                            mt = pe.n
                            kb.wait("S", pe, mt)
                            up_free[pslot] = kb.op("S", lambda s, m=m, pslot=pslot: s.activation(
                                zt[:, m * 512:(m + 1) * 512],
                                fps[:, pslot * 512:(pslot + 1) * 512], AF.Gelu), inc=ac)
                        z_tok = ac.n
                        kb.wait("T", ac, z_tok)
                        for d2 in range(2):
                            fc = 2 * h + d2
                            pslot = d2 % 2
                            if dn_free[pslot]:
                                kb.wait("T", ve, dn_free[pslot])
                            for mk in range(8):
                                kb.op("T", lambda t, sl=sl, d2=d2, mk=mk, pslot=pslot: t.matmul(
                                    fps2[:, pslot * 512:(pslot + 1) * 512],
                                    wt[:, sl * 4096 + 2048 + mk * 256 + d2 * 128: sl * 4096 + 2048 + mk * 256 + d2 * 128 + 128],
                                    zt[:, mk * 512:(mk + 1) * 512],
                                    start=(mk == 0), stop=(mk == 7)),
                                    inc=pe if mk == 7 else None)
                            mt = pe.n
                            ei = (fc * 4 + tsub) % 4
                            kb.wait_all("V", qs)
                            kb.wait("V", pe, mt)
                            et = kb.op("V", lambda v, fc=fc, tsub=tsub, pslot=pslot, ei=ei: v.tensor_tensor(
                                ot[:, ei * 512:(ei + 1) * 512],
                                fps2[:, pslot * 512:(pslot + 1) * 512],
                                STREAM[:, fc * TPC + tsub * 512: fc * TPC + (tsub + 1) * 512],
                                OP.add), inc=ve)
                            dn_free[pslot] = et
                            kb.wait("Y", ve, et)
                            kb.dma("Y", x_out_ap[fc * 128:(fc + 1) * 128, tsub * 512:(tsub + 1) * 512],
                                   ot[:, ei * 512:(ei + 1) * 512], qs)
                            ocons[ei] = qs.n
                    wcons[sl] = pe.n
                kb.flush()

        def phase_d1():
            PT["mamba_es"] = contextlib.ExitStack()
            PT["dtp"] = PT["mamba_es"].enter_context(nc.sbuf_tensor("dtp_t", [64, TPC], F32))
            PT["cum"] = PT["mamba_es"].enter_context(nc.sbuf_tensor("cum_t", [64, TPC], F32))
            PT["hf"] = PT["mamba_es"].enter_context(nc.sbuf_tensor("d1_hf", [128, 192], F32))
            dtp_t, cum_t, hf = PT["dtp"], PT["cum"], PT["hf"]
            for kc in range(NKC):
                kb.dma("Y", STREAM[:, kc * TPC:(kc + 1) * TPC],
                       x2_d[kc * 128:(kc + 1) * 128, :], qx)
            xtok = qx.n
            st_tok = emit_ln_stats(0, qx, xtok)
            ln3_tok = emit_ln_apply(0, st_tok)
            with contextlib.ExitStack() as _es:
                etile = _es.enter_context(nc.sbuf_tensor("d1_ev", [128, 4 * 512], BF16))
                hst = _es.enter_context(nc.sbuf_tensor("d1_hst", [128, 192], F32))
                kb.op("V", lambda v: v.memset(hst[:, :], 0.0), inc=ve)
                ering = [0, 0, 0, 0]

                def evac_d1(fc, tsub, ps, mm_tok):
                    ei = (fc * 4 + tsub) % 4
                    if fc < 32:
                        kb.wait_all("S", qs)
                        kb.wait("S", pe, mm_tok)
                        et = kb.op("S", lambda s, ei=ei, ps=ps: s.activation(
                            etile[:, ei * 512:(ei + 1) * 512], ps[:], AF.Silu), inc=ac)
                        kb.wait("Y", ac, et)
                        kb.dma("Y", gate_b_sp[fc, :, tsub * 512:(tsub + 1) * 512],
                               etile[:, ei * 512:(ei + 1) * 512], qs)
                        ering[ei] = qs.n
                        return ac, et
                    fi = fc - 32
                    kb.wait_all("V", qs)
                    kb.wait("V", pe, mm_tok)
                    et = kb.op("V", lambda v, ei=ei, ps=ps: v.tensor_copy(
                        etile[:, ei * 512:(ei + 1) * 512], ps[:]), inc=ve)
                    if tsub == 3:
                        et = kb.op("V", lambda v, ei=ei, fi=fi: v.tensor_copy(
                            hst[:, fi * 4 + 1: fi * 4 + 4],
                            etile[:, ei * 512 + 509: ei * 512 + 512]), inc=ve)
                    kb.wait("Y", ve, et)
                    kb.dma("Y", xbc_sp[fi, :, tsub * 512:(tsub + 1) * 512],
                           etile[:, ei * 512:(ei + 1) * 512], qs)
                    ering[ei] = qs.n
                    return ve, et

                emit_proj(inpj_w, 40, ln3_tok, evac_d1, "d1")
                kb.wait("Y", ve, ve.n)
                kb.dma("Y", cc_h_src.ap(), hst[:, :], qs)
                kb.wait_all("G", qs)
                kb.op("G", lambda g: g.collective_compute(
                    "AllGather", OP.bypass, replica_groups=[list(range(NC))],
                    ins=[cc_h_src.ap().opt()], outs=[cc_h_dst.ap().opt()]), inc=cc)
                kb.flush()
            with contextlib.ExitStack() as _es:
                gh = _es.enter_context(nc.sbuf_tensor("d1_gh", [128, NC * 192], F32))
                kb.wait_all("G", kb.sem("qw0_d1"))
                kb.wait_all("G", kb.sem("qw1_d1"))
                kb.fence("G")
                kb.wait("G", cc, cc.n)
                kb.dma("G", gh[:, :],
                       APX(cc_h_dst_h, 0, [[192, 128], [128 * 192, NC], [1, 192]]), qg)
                kb.op("V", lambda v: v.memset(hf[:, :], 0.0), inc=ve)
                kb.wait_all("V", qg)
                for j in range(NC):
                    kb.op("V", lambda v, j=j: v.scalar_tensor_tensor(
                        hf[:, :], gh[:, j * 192:(j + 1) * 192],
                        KC["halo_mask"][:, j:j + 1], hf[:, :], OP.mult, OP.add), inc=ve)
                kb.flush()
            with contextlib.ExitStack() as _es:
                dtw = _es.enter_context(nc.sbuf_tensor("d1_dtw", [128, 4 * 512], BF16))
                dtr = _es.enter_context(nc.sbuf_tensor("d1_dtr", [64, TPC], F32))
                du = _es.enter_context(nc.sbuf_tensor("d1_u", [64, 2 * TPC], F32))
                acol = _es.enter_context(nc.sbuf_tensor("d1_acol", [64, 1], F32))
                zrow = _es.enter_context(nc.sbuf_tensor("d1_zrow", [64, TPC], F32))
                lgrow = _es.enter_context(nc.sbuf_tensor("d1_lgrow", [1, 512], F32))
                dps = _es.enter_context(nc.psum_tensor("d1_dps", [64, 2 * 512], F32))
                lps = _es.enter_context(nc.psum_tensor("d1_lps", [1, 64], F32))
                dsem = kb.sem("qdt")
                kb.wait_all("G", qg)
                kb.fence("G")
                for kg in range(4):
                    kb.dma("G", dtw[:, kg * 512:(kg + 1) * 512],
                           dtpj_w[kg].rearrange("c k j -> c (k j)"), dsem)
                kb.wait_all("T", dsem)
                for tsub in range(4):
                    pslot = tsub % 2
                    for kc in range(NKC):
                        kb.op("T", lambda t, kc=kc, tsub=tsub, pslot=pslot: t.matmul(
                            dps[:, pslot * 512:(pslot + 1) * 512],
                            dtw[:, (kc // 8) * 512 + (kc % 8) * 64: (kc // 8) * 512 + (kc % 8) * 64 + 64],
                            STREAM[:, kc * TPC + tsub * 512: kc * TPC + (tsub + 1) * 512],
                            start=(kc == 0), stop=(kc == NKC - 1)),
                            inc=pe if kc == NKC - 1 else None)
                    mt = pe.n
                    kb.wait("V", pe, mt)
                    et = kb.op("V", lambda v, tsub=tsub, pslot=pslot: v.tensor_copy(
                        dtr[:, tsub * 512:(tsub + 1) * 512],
                        dps[:, pslot * 512:(pslot + 1) * 512]), inc=ve)
                    kb.wait("T", ve, et)
                kb.op("V", lambda v: v.tensor_scalar(
                    du[:, 0:TPC], dtr[:, :], KC["dtbias"][:], 0.0, OP.add, OP.add), inc=ve)
                utok = ve.n
                kb.wait("S", ve, utok)
                kb.op("S", lambda s: s.activation(du[:, TPC:], du[:, 0:TPC], AF.Abs), inc=ac)
                kb.op("S", lambda s: s.activation(du[:, TPC:], du[:, TPC:], AF.Exp, scale=-1.0), inc=ac)
                kb.op("S", lambda s: s.activation(du[:, TPC:], du[:, TPC:], AF.Ln,
                                                  bias=KC["one_col"][0:64, :]), inc=ac)
                kb.op("S", lambda s: s.activation(du[:, 0:TPC], du[:, 0:TPC], AF.Relu), inc=ac)
                stok = ac.n
                kb.wait("V", ac, stok)
                kb.op("V", lambda v: v.tensor_tensor(
                    dtp_t[:, :], du[:, 0:TPC], du[:, TPC:], OP.add), inc=ve)
                dtok = ve.n
                kb.wait("S", ve, dtok)
                atok = kb.op("S", lambda s: s.activation(acol[:], KC["alog"][:], AF.Exp), inc=ac)
                kb.wait("V", ac, atok)
                kb.op("V", lambda v: v.tensor_scalar(
                    zrow[:, :], dtp_t[:, :], acol[:], -1.0, OP.mult, OP.mult), inc=ve)
                kb.op("V", lambda v: v.memset(du[:, 0:TPC], 0.0), inc=ve)
                kb.op("V", lambda v: v.tensor_tensor_scan(
                    cum_t[:, :], zrow[:, :], du[:, 0:TPC], 0.0, OP.add, OP.add), inc=ve)
                cum_tok = ve.n
                kb.wait("T", ve, cum_tok)
                mt = kb.op("T", lambda t: t.matmul(
                    lps[:], cum_t[:, TPC - 1:TPC], KC["ident_f"][0:64, 0:64],
                    start=True, stop=True), inc=pe)
                kb.wait("V", pe, mt)
                kb.op("V", lambda v: v.memset(lgrow[:, :], 0.0), inc=ve)
                kb.op("V", lambda v: v.tensor_copy(lgrow[:, 0:64], lps[:]), inc=ve)
                kb.wait("Y", ve, ve.n)
                kb.dma("Y", APX(cc_b_src_h, 1024 * 512, [[512, 1], [1, 512]]), lgrow[:, :], qs)
                kb.flush()

        def phase_d2():
            if "dbg_cum" in debug:
                kb.wait("Y", ve, ve.n)
                kb.dma("Y", idram["dbg_cum"][0], PT["cum"][:, :], qs)
                kb.dma("Y", idram["dbg_cum"][1], PT["dtp"][:, :], qs)
            PT["sinitb_es"] = contextlib.ExitStack()
            PT["sinit_b"] = PT["sinitb_es"].enter_context(
                nc.sbuf_tensor("sinit_b", [128, 4096], BF16))
            sinit_b = PT["sinit_b"]
            dtp_t, cum_t, hf = PT["dtp"], PT["cum"], PT["hf"]
            qd2 = kb.sem("qd2")
            xt = STREAM[:, 0:24576]
            yt = STREAM[:, 24576:28672]
            xva = STREAM[:, 28672:32768]
            x2a = STREAM[:, 32768:36864]
            ema = STREAM[:, 36864:45056]
            lma = STREAM[:, 45056:53248]
            t1a = STREAM[:, 53248:57344]
            with contextlib.ExitStack() as _es:
                sst = _es.enter_context(nc.sbuf_tensor("d2_st", [128, 4096], F32))
                sbf = _es.enter_context(nc.sbuf_tensor("d2_sbf", [128, 4096], BF16))
                carry = _es.enter_context(nc.sbuf_tensor("d2_carry", [128, 192], BF16))
                ext = _es.enter_context(nc.sbuf_tensor("d2_ext", [128, 516], BF16))
                cv = _es.enter_context(nc.sbuf_tensor("d2_cv", [128, 512], BF16))
                cols = _es.enter_context(nc.sbuf_tensor("d2_cols", [128, 256], F32))
                btm = _es.enter_context(nc.sbuf_tensor("d2_btm", [128, 128], BF16))
                cbt4 = _es.enter_context(nc.sbuf_tensor("d2_cbt4", [128, 512], F32))
                ncol = _es.enter_context(nc.sbuf_tensor("d2_ncol", [128, 64], F32))
                ec_col = _es.enter_context(nc.sbuf_tensor("d2_ecol", [64, 1], F32))
                brow = _es.enter_context(nc.sbuf_tensor("d2_brow", [1, 64], F32))
                rel = _es.enter_context(nc.sbuf_tensor("d2_rel", [64, 256], F32))
                bcb = _es.enter_context(nc.sbuf_tensor("d2_bcb", [128, 64], F32))
                scps = _es.enter_context(nc.psum_tensor("d2_scps", [128, 8 * 128], F32))
                tpm4a = _es.enter_context(nc.psum_tensor("d2_tpm4a", [128, 512], F32))
                tpm4b = _es.enter_context(nc.psum_tensor("d2_tpm4b", [128, 512], F32))
                tpm42 = [tpm4a, tpm4b]
                ctp = tpm4a[:, 0:64]
                bc64 = tpm4a[:, 64:128]
                rrow = tpm4a[0:1, 448:512]
                ops8 = _es.enter_context(nc.psum_tensor("d2_ops8", [128, 512], F32))
                ips8 = _es.enter_context(nc.psum_tensor("d2_ips8", [128, 512], F32))
                tpb = _es.enter_context(nc.psum_tensor("d2_tpb", [128, 512], BF16))
                stps = _es.enter_context(nc.psum_tensor("d2_stps", [128, 512], F32))
                kb.op("V", lambda v: v.memset(sst[:, :], 0.0), inc=ve)
                kb.op("V", lambda v: v.memset(sbf[:, :], 0.0), inc=ve)
                x2a_free = 0
                kb.wait_all("V", qg)
                for fi in range(48):
                    kb.op("V", lambda v, fi=fi: v.tensor_copy(
                        carry[:, fi * 4 + 1: fi * 4 + 4], hf[:, fi * 4 + 1: fi * 4 + 4]), inc=ve)
                for tt in range(4):
                    kb.wait("Y", pe, pe.n)
                    kb.wait("Y", ve, ve.n)
                    kb.wait("Y", ac, ac.n)
                    kb.wait_all("Y", qs)
                    for fc in range(48):
                        kb.dma("Y", xt[:, fc * 512:(fc + 1) * 512],
                               xbc_sp[fc, :, tt * 512:(tt + 1) * 512], qd2)
                    kb.wait_all("V", qd2)
                    for fi in range(48):
                        kb.op("V", lambda v, fi=fi: v.tensor_copy(
                            ext[:, 1:4], carry[:, fi * 4 + 1: fi * 4 + 4]), inc=ve)
                        kb.op("V", lambda v, fi=fi: v.tensor_copy(
                            ext[:, 4:516], xt[:, fi * 512:(fi + 1) * 512]), inc=ve)
                        kb.op("V", lambda v, fi=fi: v.tensor_copy(
                            carry[:, fi * 4 + 1: fi * 4 + 4], ext[:, 513:516]), inc=ve)
                        kb.op("V", lambda v, fi=fi: v.tensor_scalar(
                            cv[:, :], ext[:, 1:513],
                            KC["convw"][:, fi * 4: fi * 4 + 1], KC["convb"][:, fi:fi + 1],
                            OP.mult, OP.add), inc=ve)
                        for k in range(1, 4):
                            kb.op("V", lambda v, fi=fi, k=k: v.scalar_tensor_tensor(
                                cv[:, :], ext[:, 1 + k:513 + k],
                                KC["convw"][:, fi * 4 + k: fi * 4 + k + 1], cv[:, :],
                                OP.mult, OP.add), inc=ve)
                        cv_tok = ve.n
                        kb.wait("S", ve, cv_tok)
                        st = kb.op("S", lambda s, fi=fi: s.activation(
                            xt[:, fi * 512:(fi + 1) * 512], cv[:, :], AF.Silu), inc=ac)
                        kb.wait("V", ac, st)
                    silu_tok = ac.n
                    kb.wait("T", ac, silu_tok)
                    if tt == 0 and "dbg_xt" in debug:
                        kb.wait("Y", ac, silu_tok)
                        for fc in range(48):
                            kb.dma("Y", idram["dbg_xt"][fc], xt[:, fc * 512:(fc + 1) * 512], qs)
                    for ch in range(4):
                        cg = tt * 4 + ch
                        t0 = cg * MC
                        kb.wait("V", ac, ac.n)
                        kb.wait_all("V", qs)
                        if cg == 0:
                            kb.op("V", lambda v, t0=t0: v.tensor_scalar(
                                rel[:, 0:128], cum_t[:, t0:t0 + 128], 0.0, 0.0,
                                OP.add, OP.add), inc=ve)
                        else:
                            kb.op("V", lambda v, t0=t0: v.tensor_scalar(
                                rel[:, 0:128], cum_t[:, t0:t0 + 128],
                                cum_t[:, t0 - 1:t0], 0.0, OP.subtract, OP.add), inc=ve)
                        kb.op("V", lambda v, t0=t0: v.tensor_scalar(
                            rel[:, 128:256], cum_t[:, t0:t0 + 128],
                            cum_t[:, t0 + 127:t0 + 128], 0.0, OP.subtract, OP.add), inc=ve)
                        rel_tok = ve.n
                        kb.wait("T", ve, rel_tok)
                        kb.wait("T", ac, ac.n)
                        mt = kb.op("T", lambda t: t.transpose(
                            ctp, rel[:, 0:128], KC["ident_f"][0:64, 0:64]), inc=pe)
                        kb.wait("V", pe, mt)
                        kb.op("V", lambda v: v.tensor_copy(cols[:, 0:64], ctp), inc=ve)
                        c1 = ve.n
                        kb.wait("S", ve, c1)
                        bt_tok = kb.op("S", lambda s: s.activation(
                            cols[:, 64:128], cols[:, 0:64], AF.Exp), inc=ac)
                        kb.wait("T", ve, c1)
                        mt = kb.op("T", lambda t: t.transpose(
                            ctp, rel[:, 128:256], KC["ident_f"][0:64, 0:64]), inc=pe)
                        kb.wait("S", pe, mt)
                        vx_tok = kb.op("S", lambda s: s.activation(
                            cols[:, 128:192], ctp, AF.Exp, scale=-1.0), inc=ac)
                        kb.wait("T", ac, vx_tok)
                        mt = kb.op("T", lambda t, t0=t0: t.transpose(
                            ctp, dtp_t[:, t0:t0 + 128], KC["ident_f"][0:64, 0:64]), inc=pe)
                        kb.wait("V", pe, mt)
                        kb.op("V", lambda v: v.tensor_copy(cols[:, 192:256], ctp), inc=ve)
                        kb.wait("V", ac, vx_tok)
                        kb.op("V", lambda v: v.tensor_tensor(
                            cols[:, 128:192], cols[:, 128:192], cols[:, 192:256], OP.mult), inc=ve)
                        kb.op("V", lambda v: v.tensor_scalar(
                            ncol[:, :], cols[:, 0:64], -1.0, 0.0, OP.mult, OP.add), inc=ve)
                        cols_tok = ve.n
                        kb.wait("S", ve, rel_tok)
                        ecx = kb.op("S", lambda s: s.activation(
                            ec_col[:], rel[:, 127:128], AF.Exp), inc=ac)
                        kb.wait("T", ac, ecx)
                        kb.wait("T", ve, cols_tok)
                        mt = kb.op("T", lambda t: t.matmul(
                            rrow, ec_col[:], KC["ident_f"][0:64, 0:64],
                            start=True, stop=True), inc=pe)
                        kb.wait("V", pe, mt)
                        brt = kb.op("V", lambda v: v.tensor_copy(brow[:], rrow), inc=ve)
                        kb.wait("T", ve, brt)
                        mt = kb.op("T", lambda t: t.matmul(
                            bc64, KC["ones_row_f"][:], brow[:],
                            start=True, stop=True), inc=pe)
                        kb.wait("V", pe, mt)
                        kb.op("V", lambda v: v.tensor_copy(bcb[:], bc64), inc=ve)
                        bcb_tok = ve.n
                        kb.wait("T", ve, ve.n)
                        for g in range(MG):
                            kb.op("T", lambda t, g=g, ch=ch: t.matmul(
                                scps[:, g * 128:(g + 1) * 128],
                                xt[:, (32 + g) * 512 + ch * 128: (32 + g) * 512 + (ch + 1) * 128],
                                xt[:, (40 + g) * 512 + ch * 128: (40 + g) * 512 + (ch + 1) * 128],
                                start=True, stop=True), inc=pe)
                        sc_tok = pe.n
                        kb.wait("T", ve, x2a_free)
                        tpb_free = 0
                        for q in range(8):
                            kb.wait("T", ve, tpb_free)
                            for p4 in range(4):
                                pr = q * 4 + p4
                                kb.op("T", lambda t, pr=pr, ch=ch: t.transpose(
                                    tpb[:, (pr % 4) * 128:((pr % 4) + 1) * 128],
                                    xt[:, pr * 512 + ch * 128: pr * 512 + (ch + 1) * 128],
                                    KC["ident_bf"][:]), inc=pe)
                            kb.wait("V", pe, pe.n)
                            for p4 in range(4):
                                pr = q * 4 + p4
                                kb.op("V", lambda v, pr=pr: v.tensor_copy(
                                    x2a[:, pr * 128:(pr + 1) * 128],
                                    tpb[:, (pr % 4) * 128:((pr % 4) + 1) * 128]), inc=ve)
                            tpb_free = ve.n
                        kb.wait("V", ve, cols_tok)
                        tpm_free = [0, 0]
                        cbt_free = 0
                        for q in range(16):
                            sl4 = q % 2
                            kb.wait("V", pe, cbt_free)
                            for q4 in range(4):
                                h = q * 4 + q4
                                kb.op("V", lambda v, h=h, q4=q4: v.tensor_scalar(
                                    cbt4[:, q4 * 128:(q4 + 1) * 128], KC["tri_negT"][:],
                                    cols[:, h:h + 1], 0.0, OP.add, OP.add), inc=ve)
                            cb_tok = ve.n
                            kb.wait("T", ve, cb_tok)
                            kb.wait("T", ac, tpm_free[sl4])
                            for q4 in range(4):
                                kb.op("T", lambda t, q4=q4, sl4=sl4: t.transpose(
                                    tpm42[sl4][:, q4 * 128:(q4 + 1) * 128],
                                    cbt4[:, q4 * 128:(q4 + 1) * 128],
                                    KC["ident_f"][:]), inc=pe)
                            cbt_free = pe.n
                            kb.wait("S", pe, pe.n)
                            for q4 in range(4):
                                h = q * 4 + q4
                                kb.op("S", lambda s, h=h, q4=q4, sl4=sl4: s.activation(
                                    ema[:, h * 128:(h + 1) * 128],
                                    tpm42[sl4][:, q4 * 128:(q4 + 1) * 128],
                                    AF.Exp, bias=ncol[:, h:h + 1]), inc=ac)
                            tpm_free[sl4] = ac.n
                        em_tok = ac.n
                        kb.wait("V", ac, em_tok)
                        kb.wait("V", pe, sc_tok)
                        for h in range(MH):
                            g = h // 8
                            kb.op("V", lambda v, h=h, g=g: v.scalar_tensor_tensor(
                                lma[:, h * 128:(h + 1) * 128],
                                scps[:, g * 128:(g + 1) * 128],
                                cols[:, 192 + h:193 + h],
                                ema[:, h * 128:(h + 1) * 128],
                                OP.mult, OP.mult), inc=ve)
                        for h in range(MH):
                            xsl = (h // 2) * 128 + (h % 2) * 64
                            kb.op("V", lambda v, h=h, xsl=xsl: v.tensor_scalar(
                                xva[:, h * 64:(h + 1) * 64], x2a[:, xsl:xsl + 64],
                                cols[:, 128 + h:129 + h], 0.0, OP.mult, OP.add), inc=ve)
                        app_tok = ve.n
                        kb.wait("T", ve, app_tok)
                        for g in range(MG):
                            for h8 in range(8):
                                h = g * 8 + h8
                                xsl = (h // 2) * 128 + (h % 2) * 64
                                kb.op("T", lambda t, h=h, h8=h8, xsl=xsl: t.matmul(
                                    ops8[:, h8 * 64:(h8 + 1) * 64],
                                    lma[:, h * 128:(h + 1) * 128],
                                    x2a[:, xsl:xsl + 64],
                                    start=True, stop=True), inc=pe)
                            for h8 in range(8):
                                h = g * 8 + h8
                                kb.op("T", lambda t, g=g, ch=ch, h=h, h8=h8: t.matmul(
                                    ips8[:, h8 * 64:(h8 + 1) * 64],
                                    xt[:, (40 + g) * 512 + ch * 128: (40 + g) * 512 + (ch + 1) * 128],
                                    sbf[:, h * 64:(h + 1) * 64],
                                    start=True, stop=True), inc=pe)
                            kb.op("T", lambda t, g=g, ch=ch: t.transpose(
                                tpb[:, 0:128], xt[:, (32 + g) * 512 + ch * 128: (32 + g) * 512 + (ch + 1) * 128],
                                KC["ident_bf"][:]), inc=pe)
                            mm_tok = pe.n
                            kb.wait("V", pe, mm_tok)
                            for h8 in range(8):
                                h = g * 8 + h8
                                kb.op("V", lambda v, h=h, h8=h8: v.tensor_scalar(
                                    t1a[:, h * 64:(h + 1) * 64], ips8[:, h8 * 64:(h8 + 1) * 64],
                                    cols[:, 64 + h:65 + h], 0.0, OP.mult, OP.add), inc=ve)
                            for h8 in range(8):
                                h = g * 8 + h8
                                kb.op("V", lambda v, h=h, h8=h8: v.tensor_tensor(
                                    t1a[:, h * 64:(h + 1) * 64], t1a[:, h * 64:(h + 1) * 64],
                                    ops8[:, h8 * 64:(h8 + 1) * 64], OP.add), inc=ve)
                            for h8 in range(8):
                                h = g * 8 + h8
                                xsl = (h // 2) * 128 + (h % 2) * 64
                                kb.op("V", lambda v, h=h, xsl=xsl: v.scalar_tensor_tensor(
                                    yt[:, h * 64:(h + 1) * 64], x2a[:, xsl:xsl + 64],
                                    KC["d_bc"][:, h:h + 1], t1a[:, h * 64:(h + 1) * 64],
                                    OP.mult, OP.add), inc=ve)
                            bt2 = kb.op("V", lambda v: v.tensor_copy(btm[:], tpb[:, 0:128]), inc=ve)
                            kb.wait("T", ve, bt2)
                            mt = kb.op("T", lambda t, g=g: t.matmul(
                                stps[:], btm[:], xva[:, g * 512:(g + 1) * 512],
                                start=True, stop=True), inc=pe)
                            kb.wait("V", pe, mt)
                            kb.wait("V", ve, bcb_tok)
                            for h8 in range(8):
                                h = g * 8 + h8
                                kb.op("V", lambda v, h=h, h8=h8: v.scalar_tensor_tensor(
                                    sst[:, h * 64:(h + 1) * 64], sst[:, h * 64:(h + 1) * 64],
                                    bcb[:, h:h + 1], stps[:, h8 * 64:(h8 + 1) * 64],
                                    OP.mult, OP.add), inc=ve)
                            kb.wait("T", ve, ve.n)
                        x2a_free = ve.n
                        kb.op("V", lambda v: v.tensor_copy(sbf[:, :], sst[:, :]), inc=ve)
                        y_tok = ve.n
                        kb.wait("Y", ve, y_tok)
                        kb.dma("Y", y_sp[cg], yt, qs)
                kb.flush()
                kb.wait("Y", ve, ve.n)
                for g in range(MG):
                    kb.dma("Y", APX(cc_b_src_h, g * 128 * 512, [[512, 128], [1, 512]]),
                           sst[:, g * 512:(g + 1) * 512], qs)
                kb.flush()
            with contextlib.ExitStack() as _es:
                lg8 = _es.enter_context(nc.sbuf_tensor("d2_lg8", [NC, 64], F32))
                w8 = _es.enter_context(nc.sbuf_tensor("d2_w8", [NC, 64], F32))
                w8t = _es.enter_context(nc.sbuf_tensor("d2_w8t", [64, NC], F32))
                wrow = _es.enter_context(nc.sbuf_tensor("d2_wrow", [1, 64], F32))
                wb = _es.enter_context(nc.sbuf_tensor("d2_wb", [128, 64], F32))
                gth = _es.enter_context(nc.sbuf_tensor("d2_gth", [128, 4096], BF16))
                wps = _es.enter_context(nc.psum_tensor("d2_wps", [NC, 64], F32))
                wtp = _es.enter_context(nc.psum_tensor("d2_wtp", [64, NC], F32))
                rps = _es.enter_context(nc.psum_tensor("d2_rps", [1, 64], F32))
                bps = _es.enter_context(nc.psum_tensor("d2_bps", [128, 64], F32))
                kb.wait_all("G", qs)
                kb.wait_all("G", kb.sem("qd2"))
                kb.fence("G")
                gtok = kb.op("G", lambda g: g.collective_compute(
                    "AllGather", OP.bypass, replica_groups=[list(range(NC))],
                    ins=[cc_b_src.opt()], outs=[cc_b_dst.ap().opt()]), inc=cc)
                kb.wait("G", cc, gtok)
                kb.dma("G", lg8[:, :],
                       APX(cc_b_dst_h, 1024 * 512, [[1025 * 512, NC], [1, 64]]), qg)
                kb.wait_all("T", qg)
                mt = kb.op("T", lambda t: t.matmul(
                    wps[:], KC["mmask"][:, :], lg8[:, :], start=True, stop=True), inc=pe)
                kb.wait("S", pe, mt)
                wtok = kb.op("S", lambda s: s.activation(w8[:, :], wps[:], AF.Exp), inc=ac)
                kb.wait("V", ac, wtok)
                jt = kb.op("V", lambda v: v.tensor_scalar(
                    w8[:, :], w8[:, :], KC["jmask8"][:, :], 0.0, OP.mult, OP.add), inc=ve)
                kb.wait("T", ve, jt)
                mt = kb.op("T", lambda t: t.transpose(
                    wtp[:], w8[:, :], KC["ident_f"][0:NC, 0:NC]), inc=pe)
                kb.wait("V", pe, mt)
                w8t_tok = kb.op("V", lambda v: v.tensor_copy(w8t[:, :], wtp[:]), inc=ve)
                kb.op("V", lambda v: v.memset(sinit_b[:, :], 0.0), inc=ve)
                kb.wait("T", ve, w8t_tok)
                for j in range(NC):
                    mt = kb.op("T", lambda t, j=j: t.matmul(
                        rps[:], w8t[:, j:j + 1], KC["ident_f"][0:64, 0:64],
                        start=True, stop=True), inc=pe)
                    kb.wait("V", pe, mt)
                    rt = kb.op("V", lambda v: v.tensor_copy(wrow[:, :], rps[:]), inc=ve)
                    kb.wait("T", ve, rt)
                    mt = kb.op("T", lambda t: t.matmul(
                        bps[:], KC["ones_row_f"][:], wrow[:, :], start=True, stop=True), inc=pe)
                    kb.wait("V", pe, mt)
                    kb.op("V", lambda v: v.tensor_copy(wb[:, :], bps[:]), inc=ve)
                    kb.wait("G", ve, ve.n)
                    kb.dma("G", gth[:, :],
                           APX(cc_b_dst_h, j * 1025 * 512,
                               [[512, 128], [128 * 512, 8], [1, 512]]), qg)
                    kb.wait_all("V", qg)
                    for h in range(MH):
                        kb.op("V", lambda v, h=h: v.scalar_tensor_tensor(
                            sinit_b[:, h * 64:(h + 1) * 64], gth[:, h * 64:(h + 1) * 64],
                            wb[:, h:h + 1], sinit_b[:, h * 64:(h + 1) * 64],
                            OP.mult, OP.add), inc=ve)
                kb.flush()

        def phase_e():
            sinit_b = PT["sinit_b"]
            dtp_t, cum_t = PT["dtp"], PT["cum"]
            qe = kb.sem("qe")
            qy = kb.sem("qy")
            with contextlib.ExitStack() as _es:
                ecum = _es.enter_context(nc.sbuf_tensor("e_ec", [64, TPC], F32))
                ectm = _es.enter_context(nc.sbuf_tensor("e_ecT", [128, 16 * 64], F32))
                ct = _es.enter_context(nc.sbuf_tensor("e_ct", [128, 8 * 512], BF16))
                ytl = _es.enter_context(nc.sbuf_tensor("e_y", [128, 4096], BF16))
                yf = ytl
                sq = _es.enter_context(nc.sbuf_tensor("e_sq", [128, 512], BF16))
                rr8 = _es.enter_context(nc.sbuf_tensor("e_r", [128, 8], F32))
                cps = _es.enter_context(nc.psum_tensor("e_cps", [128, 2 * 512], F32))
                etp = _es.enter_context(nc.psum_tensor("e_tp", [128, 64], F32))
                tpb_a = _es.enter_context(nc.psum_tensor("e_tpb_a", [128, 512], BF16))
                tpb_b = _es.enter_context(nc.psum_tensor("e_tpb_b", [128, 512], BF16))
                tpb2 = [tpb_a, tpb_b]
                kb.wait("S", ve, ve.n)
                ec_tok = kb.op("S", lambda s: s.activation(ecum[:, :], cum_t[:, :], AF.Exp), inc=ac)
                kb.wait("T", ac, ec_tok)
                for tc in range(16):
                    mt = kb.op("T", lambda t, tc=tc: t.transpose(
                        etp[:], ecum[:, tc * 128:(tc + 1) * 128], KC["ident_f"][0:64, 0:64]), inc=pe)
                    kb.wait("V", pe, mt)
                    et = kb.op("V", lambda v, tc=tc: v.tensor_copy(
                        ectm[:, tc * 64:(tc + 1) * 64], etp[:]), inc=ve)
                    kb.wait("T", ve, et)
                ycons = 0
                ycons_pe = 0
                for tt in range(4):
                    kb.wait("Y", pe, pe.n)
                    for g in range(8):
                        kb.dma("Y", ct[:, g * 512:(g + 1) * 512],
                               xbc_sp[40 + g, :, tt * 512:(tt + 1) * 512], qe)
                    for ch in range(4):
                        cg = tt * 4 + ch
                        kb.wait("Y", ve, ycons)
                        kb.wait("Y", pe, ycons_pe)
                        kb.dma("Y", ytl[:, :], y_sp[cg], qy)
                        kb.wait_all("T", qe)
                        kb.wait_all("V", qy)
                        for g in range(MG):
                            mt = kb.op("T", lambda t, g=g, ch=ch: t.matmul(
                                cps[:, (g % 2) * 512:(g % 2) * 512 + 512],
                                ct[:, g * 512 + ch * 128: g * 512 + (ch + 1) * 128],
                                sinit_b[:, g * 512:(g + 1) * 512],
                                start=True, stop=True), inc=pe)
                            kb.wait("V", pe, mt)
                            for h8 in range(8):
                                h = g * 8 + h8
                                kb.op("V", lambda v, g=g, h=h, h8=h8, cg=cg: v.scalar_tensor_tensor(
                                    yf[:, h * 64:(h + 1) * 64],
                                    cps[:, (g % 2) * 512 + h8 * 64: (g % 2) * 512 + (h8 + 1) * 64],
                                    ectm[:, cg * 64 + h: cg * 64 + h + 1],
                                    ytl[:, h * 64:(h + 1) * 64],
                                    OP.mult, OP.add), inc=ve)
                            kb.wait("T", ve, ve.n)
                        ycons = ve.n
                        for g in range(MG):
                            kb.op("V", lambda v, g=g: v.tensor_tensor(
                                sq[:, :], yf[:, g * 512:(g + 1) * 512],
                                yf[:, g * 512:(g + 1) * 512], OP.mult), inc=ve)
                            kb.op("V", lambda v, g=g: v.reduce_sum(
                                rr8[:, g:g + 1], sq[:, :], axis=AX.X), inc=ve)
                        rtok = ve.n
                        kb.wait("S", ve, rtok)
                        kb.op("S", lambda s: s.activation(rr8[:, :], rr8[:, :], AF.Ln,
                                                          bias=KC["eps_col"][:], scale=1.0 / 512.0), inc=ac)
                        sa = kb.op("S", lambda s: s.activation(rr8[:, :], rr8[:, :], AF.Exp,
                                                               scale=-0.5), inc=ac)
                        kb.wait("V", ac, sa)
                        for g in range(MG):
                            kb.op("V", lambda v, g=g: v.tensor_scalar(
                                yf[:, g * 512:(g + 1) * 512], yf[:, g * 512:(g + 1) * 512],
                                rr8[:, g:g + 1], 0.0, OP.mult, OP.add), inc=ve)
                        yn_tok = ve.n
                        kb.wait("T", ve, yn_tok)
                        for fq in range(8):
                            if fq >= 2:
                                kb.wait("T", ve, ve.n)
                            for f4 in range(4):
                                fc = fq * 4 + f4
                                kb.op("T", lambda t, fc=fc, fq=fq, f4=f4: t.transpose(
                                    tpb2[fq % 2][:, f4 * 128:(f4 + 1) * 128],
                                    yf[:, fc * 128:(fc + 1) * 128], KC["ident_bf"][:]), inc=pe)
                            kb.wait("V", pe, pe.n)
                            for f4 in range(4):
                                fc = fq * 4 + f4
                                kb.op("V", lambda v, fc=fc, cg=cg, fq=fq, f4=f4: v.tensor_scalar(
                                    STREAM[:, fc * TPC + cg * 128: fc * TPC + (cg + 1) * 128],
                                    tpb2[fq % 2][:, f4 * 128:(f4 + 1) * 128],
                                    KC["norm_b"][:, fc:fc + 1], 0.0, OP.mult, OP.add), inc=ve)
                        ycons = ve.n
                        ycons_pe = pe.n
                kb.flush()
            with contextlib.ExitStack() as _es:
                gt = _es.enter_context(nc.sbuf_tensor("e_g", [128, 2 * TPC], BF16))
                gsem = [kb.sem("qeg0"), kb.sem("qeg1")]
                kb.wait_all("Y", qe)
                kb.wait_all("Y", qy)
                kb.fence("Y")
                cons = [0, 0]
                for fc in range(NKC):
                    sl = fc % 2
                    kb.wait("Y", ve, cons[sl])
                    kb.dma("Y", gt[:, sl * TPC:(sl + 1) * TPC], gate_b_sp[fc], gsem[sl])
                    kb.wait_all("V", gsem[sl])
                    kb.op("V", lambda v, fc=fc, sl=sl: v.tensor_tensor(
                        STREAM[:, fc * TPC:(fc + 1) * TPC],
                        STREAM[:, fc * TPC:(fc + 1) * TPC],
                        gt[:, sl * TPC:(sl + 1) * TPC], OP.mult), inc=ve)
                    cons[sl] = ve.n
                rhs_tok = ve.n
                kb.flush()
            PT["sinitb_es"].close()
            PT["mamba_es"].close()
            with contextlib.ExitStack() as _es:
                x2c = _es.enter_context(nc.sbuf_tensor("e_x2", [128, TPC], BF16))
                x3t = _es.enter_context(nc.sbuf_tensor("e_x3", [128, 4 * 512], BF16))
                xsem = kb.sem("qex")
                kb.wait_all("Y", kb.sem("qeg0"))
                kb.wait_all("Y", kb.sem("qeg1"))
                kb.fence("Y")
                st = {"xcons": 0}
                ering = [0, 0, 0, 0]

                def evac_e3(fc, tsub, ps, mm_tok):
                    if tsub == 0:
                        kb.wait("Y", ve, st["xcons"])
                        kb.dma("Y", x2c[:, :], x2_d[fc * 128:(fc + 1) * 128, :], xsem)
                    ei = (fc * 4 + tsub) % 4
                    kb.wait_all("V", qs)
                    kb.wait_all("V", xsem)
                    kb.wait("V", pe, mm_tok)
                    et = kb.op("V", lambda v, ps=ps, tsub=tsub, ei=ei: v.tensor_tensor(
                        x3t[:, ei * 512:(ei + 1) * 512], ps[:],
                        x2c[:, tsub * 512:(tsub + 1) * 512], OP.add), inc=ve)
                    if tsub == 3:
                        st["xcons"] = ve.n
                    kb.wait("Y", ve, et)
                    kb.dma("Y", x3_d[fc * 128:(fc + 1) * 128, tsub * 512:(tsub + 1) * 512],
                           x3t[:, ei * 512:(ei + 1) * 512], qs)
                    ering[ei] = qs.n
                    return ve, et

                emit_proj(outp_w, 16, rhs_tok, evac_e3, "e3")
                kb.flush()

        if LAST >= 1:
            phase_a2()
            kb.barrier()
        if LAST >= 2:
            phase_b()
            kb.barrier()
        if LAST >= 3:
            phase_ffn(x1_d, x2_d, "c")
            kb.barrier()
        if LAST >= 4:
            phase_d1()
            kb.barrier()
        if LAST >= 5:
            phase_d2()
            kb.barrier()
        if LAST >= 6:
            phase_e()
            kb.barrier()
        if LAST >= 7:
            phase_ffn(x3_d, out_fm, "f", final=True)
            kb.barrier()

        if upto != "F":
            with nc.sbuf_tensor("zz", [128, 512], F32) as zz:
                kb.op("V", lambda v: v.memset(zz[:], 0.0), inc=ve)
                kb.wait("Y", ve, ve.n)
                for kc in range(NKC):
                    for tsub in range(4):
                        kb.dma("Y", out_fm[kc * 128:(kc + 1) * 128, tsub * 512:(tsub + 1) * 512],
                               zz[:], qs)
                kb.flush()
        kb.wait("Y", qs, qs.n)
        kb.wait("Y", qw, qw.n)
        kb.flush()
        for _k in ("sinitb_es", "mamba_es", "sinit_es"):
            if _k in PT:
                PT[_k].close()
        kb.ctx.close()
    return nc

def kernel(**inputs):
    in_maps, perm = _host_prep(inputs)
    nc = build_program()
    res = run_bass_kernel_spmd(nc, in_maps, core_ids=list(range(NC)))
    inv = np.argsort(perm)
    full = np.concatenate([np.asarray(r["out_fm"]).T for r in res.results], axis=0)
    return full[:, inv][None].astype(np.float32)
